# version 05 (final)
# speedup vs baseline: 1.4409x; 1.0030x over previous
import numpy as np
from contextlib import ExitStack

import concourse.bass as bass
import concourse.mybir as mybir
from concourse.bass import IndirectOffsetOnAxis

F32 = mybir.dt.float32
BF16 = mybir.dt.bfloat16
I32 = mybir.dt.int32
AF = mybir.ActivationFunctionType
ALU = mybir.AluOpType
AX = mybir.AxisListType

ENGINES = ("tensor", "vector", "scalar", "gpsimd", "sync")

D = 1024
NC_D = 8
T = 2048
S = 4096
NH = 16
DH = 64


class KB:

    def __init__(self):
        self.nc = bass.Bass("TRN2", target_bir_lowering=False)
        self.stack = ExitStack()
        self.scopes = [self.stack]
        self.thunks = {e: [] for e in ENGINES}
        self.semh = {}
        self.semc = {}
        self.n_out_dma = 0
        self._uid = 0
        self.io = {}
        self.pfx = ""

    def uid(self):
        self._uid += 1
        return self._uid

    def push_scope(self):
        self.scopes.append(ExitStack())

    def pop_scope(self):
        self.scopes.pop().close()

    def sb(self, name, shape, dtype):
        return self.scopes[-1].enter_context(
            self.nc.sbuf_tensor(f"{name}_{self.uid()}", shape, dtype))

    def psum(self, name, shape, dtype=F32):
        return self.scopes[-1].enter_context(
            self.nc.psum_tensor(f"{name}_{self.uid()}", shape, dtype))

    def dram(self, name, shape, dtype, kind="Internal", addr_space="Local"):
        return self.nc.dram_tensor(name, shape, dtype, kind=kind, addr_space=addr_space)

    def inp(self, name, shape, dtype=F32):
        if name in self.io:
            return self.io[name]
        return self.nc.dram_tensor(self.pfx + name, shape, dtype,
                                   kind="ExternalInput")

    def outp(self, name, shape, dtype=F32):
        if name in self.io:
            return self.io[name]
        return self.nc.dram_tensor(self.pfx + name, shape, dtype,
                                   kind="ExternalOutput")

    def sem(self, name):
        if name not in self.semh:
            self.semh[name] = self.stack.enter_context(self.nc.semaphore(name))
            self.semc[name] = 0
        return self.semh[name]

    def emit(self, eng, fn):
        self.thunks[eng].append(fn)

    def dma(self, q, out, in_, sem=None, **kw):
        semname = sem or ("qS" if q == "sync" else "qG")
        s = self.sem(semname)
        self.semc[semname] += 16
        v = self.semc[semname]
        self.emit(q, lambda e, out=out, in_=in_, s=s, kw=kw:
                  e.dma_start(out=out, in_=in_, **kw).then_inc(s, 16))
        return v

    def wait(self, eng, semname, val=None):
        s = self.sem(semname)
        v = self.semc[semname] if val is None else val
        if v > 0:
            self.emit(eng, lambda e, s=s, v=v: e.wait_ge(s, v))

    def op(self, eng, fn, inc=None, k=1):
        if inc is None:
            self.emit(eng, fn)
            return None
        s = self.sem(inc)
        self.semc[inc] += k
        v = self.semc[inc]
        self.emit(eng, lambda e, fn=fn, s=s, k=k: fn(e).then_inc(s, k))
        return v

    def cc(self, kind, op_, groups, ins, outs):
        s = self.sem("CC")
        self.semc["CC"] += 1
        v = self.semc["CC"]
        self.emit("gpsimd", lambda e, kind=kind, op_=op_, groups=groups, ins=ins,
                  outs=outs, s=s:
                  e.collective_compute(kind, op_, replica_groups=groups,
                                       ins=ins, outs=outs).then_inc(s, 1))
        return v

    def seq(self, eng, *fns, sem="SEQ", final_inc=None):
        for i, f in enumerate(fns):
            self.wait(eng, sem)
            if final_inc is not None and i == len(fns) - 1:
                self.op(eng, f, inc=final_inc)
            else:
                self.op(eng, f, inc=sem)

    def barrier(self, engines=ENGINES):
        sems_now = [(n, v) for n, v in self.semc.items()
                    if v > 0 and not n.startswith("B")]
        for eng in engines:
            for n, v in sems_now:
                self.wait(eng, n, v)
        snap = {}
        for eng in engines:
            bname = "B" + eng[0]
            s = self.sem(bname)
            self.semc[bname] += 1
            snap[bname] = self.semc[bname]
            self.emit(eng, lambda e, s=s: e.nop().then_inc(s, 1))
        for eng in engines:
            for bname, v in snap.items():
                s = self.semh[bname]
                self.emit(eng, lambda e, s=s, v=v: e.wait_ge(s, v))

    def build(self):
        with self.nc.Block() as blk:
            for eng in ENGINES:
                ths = self.thunks[eng]
                getattr(blk, eng)(lambda e, ths=ths: [f(e) for f in ths])
        self.stack.close()
        return self.nc



def bounds_reg(value):
    hold = {}

    def get(e):
        if "r" not in hold:
            hold["r"] = e.to_reg(value)
        return hold["r"]
    return get

def load_x(kb, x_dram, x_sb, t=T):
    kb.dma("sync", x_sb[:].rearrange("p (c t) -> p c t", c=NC_D),
           x_dram.ap().rearrange("(c p) t -> p c t", p=128))


def store_x(kb, x_dram, x_sb, t=T):
    kb.dma("sync", x_dram.ap().rearrange("(c p) t -> p c t", p=128),
           x_sb[:].rearrange("p (c t) -> p c t", c=NC_D))


def emit_ln(kb, x_sb, xhat_sb, aux, t=T, want_stats=False):
    nt = t // 512
    xb, sq = aux["xb"], aux["sq"]
    rows = aux["rows"]
    m_r = rows[0:1, 0:t]
    e2_r = rows[0:1, t:2 * t]
    rstd_r = rows[0:1, 2 * t:3 * t]
    mr_r = rows[0:1, 3 * t:4 * t]
    ps_rowA, ps_rowB = aux["ps_rowA"], aux["ps_rowB"]
    ps_bcA, ps_bcB = aux["ps_bcA"], aux["ps_bcB"]

    for dc in range(NC_D):
        kb.op("vector", lambda e, dc=dc: e.tensor_copy(
            xb[:, dc * t:(dc + 1) * t], x_sb[:, dc * t:(dc + 1) * t]), inc="V")
        kb.op("scalar", lambda e, dc=dc: e.activation(
            sq[:, dc * t:(dc + 1) * t], x_sb[:, dc * t:(dc + 1) * t], AF.Square),
            inc="S")
    vcast, ssq = kb.semc["V"], kb.semc["S"]
    kb.wait("tensor", "V", vcast)
    kb.wait("tensor", "S", ssq)
    for tt in range(nt):
        for dc in range(NC_D):
            kb.op("tensor", lambda e, tt=tt, dc=dc: e.matmul(
                ps_rowA[0:1, 0:512], aux["ones_bf"][:, 0:1],
                xb[:, dc * t + tt * 512: dc * t + (tt + 1) * 512],
                start=(dc == 0), stop=(dc == NC_D - 1)))
            kb.op("tensor", lambda e, tt=tt, dc=dc: e.matmul(
                ps_rowB[0:1, 0:512], aux["ones_bf"][:, 0:1],
                sq[:, dc * t + tt * 512: dc * t + (tt + 1) * 512],
                start=(dc == 0), stop=(dc == NC_D - 1)),
                inc=("T" if dc == NC_D - 1 else None))
        tv = kb.semc["T"]
        kb.wait("vector", "T", tv)
        kb.op("vector", lambda e, tt=tt: e.tensor_scalar_mul(
            m_r[0:1, tt * 512:(tt + 1) * 512], ps_rowA[0:1, 0:512], 1.0 / D))
        kb.op("vector", lambda e, tt=tt: e.tensor_scalar_mul(
            e2_r[0:1, tt * 512:(tt + 1) * 512], ps_rowB[0:1, 0:512], 1.0 / D),
            inc="V")
        kb.wait("tensor", "V", kb.semc["V"])
    kb.wait("vector", "V")
    kb.seq("vector",
           lambda e: e.tensor_mul(rstd_r, m_r, m_r),
           lambda e: e.tensor_sub(e2_r, e2_r, rstd_r))
    kb.wait("scalar", "SEQ")
    kb.seq("scalar",
           lambda e: e.activation(rstd_r, e2_r, AF.Ln, bias=aux["eps"][0:1, 0:1]),
           lambda e: e.activation(rstd_r, rstd_r, AF.Exp, scale=-0.5))
    kb.wait("vector", "SEQ")
    kb.seq("vector", lambda e: e.tensor_mul(mr_r, m_r, rstd_r))
    kb.wait("tensor", "SEQ")
    rb, mrb, tmp = aux["rb"], aux["mrb"], aux["tmp"]
    for tt in range(nt):
        kb.op("tensor", lambda e, tt=tt: e.matmul(
            ps_bcA[:, 0:512], aux["ones_f1"][0:1, :],
            rstd_r[0:1, tt * 512:(tt + 1) * 512], start=True, stop=True))
        kb.op("tensor", lambda e, tt=tt: e.matmul(
            ps_bcB[:, 0:512], aux["ones_f1"][0:1, :],
            mr_r[0:1, tt * 512:(tt + 1) * 512], start=True, stop=True), inc="T")
        tv = kb.semc["T"]
        kb.wait("vector", "T", tv)
        kb.op("vector", lambda e, tt=tt: e.tensor_copy(
            rb[:, tt * 512:(tt + 1) * 512], ps_bcA[:, 0:512]))
        kb.op("vector", lambda e, tt=tt: e.tensor_copy(
            mrb[:, tt * 512:(tt + 1) * 512], ps_bcB[:, 0:512]), inc="V")
        kb.wait("tensor", "V", kb.semc["V"])
    for dc in range(NC_D):
        kb.wait("vector", "V")
        kb.seq("vector",
               lambda e, dc=dc: e.tensor_mul(
                   tmp[:, 0:t], x_sb[:, dc * t:(dc + 1) * t], rb[:, 0:t]),
               lambda e, dc=dc: e.tensor_sub(
                   xhat_sb[:, dc * t:(dc + 1) * t], tmp[:, 0:t], mrb[:, 0:t]),
               final_inc="V")
    if want_stats:
        return m_r, rstd_r
    return None


def alloc_ln_aux(kb, t=T, sq=None):
    aux = {
        "ones_bf": kb.sb("ones_bf", [128, 1], BF16),
        "ones_f1": kb.sb("ones_f1", [1, 128], F32),
        "xb": kb.sb("ln_xb", [128, NC_D * t], BF16),
        "sq": sq if sq is not None else kb.sb("ln_sq", [128, NC_D * t], BF16),
        "rows": kb.sb("ln_rows", [1, 4 * t], F32),
        "rb": kb.sb("ln_rb", [128, t], F32),
        "mrb": kb.sb("ln_mrb", [128, t], F32),
        "tmp": kb.sb("ln_tmp", [128, t], F32),
        "ps_rowA": kb.psum("ps_rowA", [1, 512]),
        "ps_rowB": kb.psum("ps_rowB", [1, 512]),
        "ps_bcA": kb.psum("ps_bcA", [128, 512]),
        "ps_bcB": kb.psum("ps_bcB", [128, 512]),
        "eps": kb.sb("ln_eps", [1, 1], F32),
    }
    kb.op("vector", lambda e: e.memset(aux["eps"][:], 1e-5))
    kb.op("vector", lambda e: e.memset(aux["ones_bf"][:], 1.0))
    kb.op("vector", lambda e: e.memset(aux["ones_f1"][:], 1.0), inc="V")
    return aux


def load_weight_into(kb, wsb, w_dram, kin, mout):
    semname = "qL%d" % kb.uid()
    v = kb.dma("gpsimd", wsb[:].rearrange("p (c m) -> p c m", c=kin // 128),
               w_dram.ap().rearrange("(c p) m -> p c m", p=128), sem=semname)
    return semname, v


def load_weight_bf16(kb, w_dram, name, kin, mout):
    wsb = kb.sb(name, [128, (kin // 128) * mout], BF16)
    semname = "qL%d" % kb.uid()
    v = kb.dma("gpsimd", wsb[:].rearrange("p (c m) -> p c m", c=kin // 128),
               w_dram.ap().rearrange("(c p) m -> p c m", p=128), sem=semname)
    return wsb, semname, v


def wslice(wsb, kin, mout, kc, mc, mw=128):
    return wsb[:, kc * mout + mc * mw: kc * mout + mc * mw + mw]


def emit_matmul(kb, wsb, kin, mout, rhs_sb, t, epilogue, ps_list,
                rhs_col0=0, n_tile=512, mw=128, rhs_t=None):
    rhs_t = rhs_t if rhs_t is not None else t
    nkc = kin // 128
    nmc = mout // mw
    nt = t // n_tile
    pending = {}
    i = 0
    for mc in range(nmc):
        for tt in range(nt):
            ps = ps_list[i % len(ps_list)]
            if (i % len(ps_list)) in pending:
                kb.wait("tensor", "E", pending[i % len(ps_list)])
            else:
                kb.wait("tensor", "E")
            for kc in range(nkc):
                kb.op("tensor", lambda e, ps=ps, kc=kc, mc=mc, tt=tt: e.matmul(
                    ps[0:mw, 0:n_tile],
                    wslice(wsb, kin, mout, kc, mc, mw),
                    rhs_sb[:, kc * rhs_t + rhs_col0 + tt * n_tile:
                           kc * rhs_t + rhs_col0 + (tt + 1) * n_tile],
                    start=(kc == 0), stop=(kc == nkc - 1)),
                    inc=("T" if kc == nkc - 1 else None))
            tv = kb.semc["T"]
            epilogue(kb, ps, mc, tt, tv)
            pending[i % len(ps_list)] = kb.semc["E"]
            i += 1


def make_copy_epilogue(dst_sb, t, eng="scalar", n_tile=512, bias_sb=None,
                       scale=1.0, func=None, mw=128):
    f = func or AF.Identity

    def ep(kb, ps, mc, tt, tv):
        kb.wait(eng, "T", tv)
        if bias_sb is not None:
            kb.op(eng, lambda e, ps=ps, mc=mc, tt=tt: e.activation(
                dst_sb[:, mc * t + tt * n_tile: mc * t + (tt + 1) * n_tile],
                ps[:, 0:n_tile], f, bias=bias_sb[:, mc:mc + 1], scale=scale),
                inc="E")
        else:
            kb.op(eng, lambda e, ps=ps, mc=mc, tt=tt: (
                e.activation(
                    dst_sb[:, mc * t + tt * n_tile: mc * t + (tt + 1) * n_tile],
                    ps[:, 0:n_tile], f, scale=scale)
                if (scale != 1.0 or func or eng == "scalar") else
                e.tensor_copy(
                    dst_sb[:, mc * t + tt * n_tile: mc * t + (tt + 1) * n_tile],
                    ps[:, 0:n_tile])), inc="E")
    return ep


def emit_transpose_fm_to_tok(kb, src_fm, dst_tok, t, d=D, dtype=BF16,
                             ident=None, ps_tp=None):
    ntc = t // 128
    for dc in range(d // 128):
        for tc in range(ntc):
            kb.op("tensor", lambda e, dc=dc, tc=tc: e.transpose(
                ps_tp[:, 0:128],
                src_fm[:, dc * t + tc * 128: dc * t + tc * 128 + 128],
                ident[:, 0:128]), inc="T")
            tv = kb.semc["T"]
            kb.wait("vector", "T", tv)
            kb.op("vector", lambda e, dc=dc, tc=tc: e.tensor_copy(
                dst_tok[:, tc * d + dc * 128: tc * d + dc * 128 + 128],
                ps_tp[:, 0:128]), inc="E")
            kb.wait("tensor", "E", kb.semc["E"])


def emit_transpose_to_dram(kb, src_fm, out_dram, t, d=D, ident=None, ps_tp=None):
    ntc = t // 128
    stg = [kb.sb("tp_stg", [128, d], BF16) for _ in range(2)]
    for tc in range(ntc):
        j = tc % 2
        if tc >= 2:
            kb.wait("vector", "qTp")
        for dc in range(d // 128):
            kb.op("tensor", lambda e, dc=dc, tc=tc: e.transpose(
                ps_tp[:, 0:128],
                src_fm[:, dc * t + tc * 128: dc * t + tc * 128 + 128],
                ident[:, 0:128]), inc="T")
            tv = kb.semc["T"]
            kb.wait("vector", "T", tv)
            kb.op("vector", lambda e, dc=dc, j=j: e.tensor_copy(
                stg[j][:, dc * 128:(dc + 1) * 128], ps_tp[:, 0:128]), inc="E")
            kb.wait("tensor", "E", kb.semc["E"])
        kb.wait("sync", "E", kb.semc["E"])
        kb.dma("sync", out_dram.ap()[tc * 128:(tc + 1) * 128, :],
               stg[j][:, 0:d], sem="qTp")
    kb.wait("sync", "qTp")


def make_dram_epilogue(kb, out_dram, t, eng="scalar", n_tile=512, bias_sb=None,
                       scale=1.0, func=None, dtype=BF16, mw=128, q="sync"):
    f = func or AF.Identity
    tiles = [kb.sb("ep_dram", [128, n_tile], dtype) for _ in range(2)]
    state = {"i": 0, "dma_done": {}}
    oap = out_dram.ap()

    def ep(kb, ps, mc, tt, tv):
        i = state["i"]
        tile = tiles[i % 2]
        if i >= 2:
            kb.wait(eng, "qDo")
        kb.wait(eng, "T", tv)
        if bias_sb is not None:
            kb.op(eng, lambda e, ps=ps, mc=mc, tile=tile: e.activation(
                tile[0:mw, 0:n_tile], ps[0:mw, 0:n_tile], f,
                bias=bias_sb[:, mc:mc + 1], scale=scale), inc="E")
        elif scale != 1.0 or func is not None:
            kb.op(eng, lambda e, ps=ps, tile=tile: e.activation(
                tile[0:mw, 0:n_tile], ps[0:mw, 0:n_tile], f, scale=scale), inc="E")
        else:
            kb.op(eng, lambda e, ps=ps, tile=tile: (
                e.copy(tile[0:mw, 0:n_tile], ps[0:mw, 0:n_tile])
                if eng == "scalar" else
                e.tensor_copy(tile[0:mw, 0:n_tile], ps[0:mw, 0:n_tile])), inc="E")
        ev = kb.semc["E"]
        kb.wait(q, "E", ev)
        state["dma_done"][i % 2] = kb.dma(
            q, oap[mc * mw:(mc + 1) * mw, tt * n_tile:(tt + 1) * n_tile],
            tile[0:mw, 0:n_tile], sem="qDo")
        state["i"] += 1
    return ep


def emit_proj_to_dram(kb, w_dram, kin, mout, rhs_sb, t, out_dram, ps_list,
                      eng="scalar", bias_sb=None, scale=1.0, func=None,
                      dtype=BF16, wname="w"):
    wsb, wsem, wv = load_weight_bf16(kb, w_dram, wname, kin, mout)
    kb.wait("tensor", wsem, wv)
    ep = make_dram_epilogue(kb, out_dram, t, eng=eng, bias_sb=bias_sb,
                            scale=scale, func=func, dtype=dtype)
    emit_matmul(kb, wsb, kin, mout, rhs_sb, t, ep, ps_list)
    return wsb


def load_bias_cols(kb, b_dram, name, n=D):
    bsb = kb.sb(name, [128, n // 128], F32)
    kb.dma("sync", bsb[:], b_dram.ap().rearrange("(c p) -> p c", p=128),
           allow_slow_non_contiguous=True)
    return bsb



def emit_attention(kb, *, q_dram, kT_dram, ktok_dram, x_dram, xout_dram,
                   sel_dram, t=T, s=S, qtile=512, decay=None, wo_dram=None,
                   exp_scale=1.0, kT_ranked=False):
    nqt = t // qtile
    nkt = s // 128
    NG = 4
    nxt = t // 512
    for _n in ("Ex", "A", "E", "E2", "V", "T", "S", "SEQ", "Dv"):
        kb.sem(_n)
    ones_f1 = kb.sb("att_ones1", [1, 128], F32)
    ones_b1 = kb.sb("att_ones1b", [1, 128], BF16)
    kb.op("vector", lambda e: e.memset(ones_f1[:], 1.0))
    kb.op("vector", lambda e: e.memset(ones_b1[:], 1.0), inc="V")
    av_T = kb.sb("att_av", [128, NC_D * t], BF16)
    rs_row = kb.sb("att_rs", [1, NG * qtile], F32)
    kb.sb("att_pad", [1, 16], F32)
    ps_bc = kb.psum("att_ps_bc", [128, 512])
    rs_d = kb.dram("att_rs_d_%d" % kb.uid(), [16, t], F32)

    kb.push_scope()
    ps_s = [kb.psum("att_ps_s", [128, qtile]) for _ in range(2)]
    ps_av = [kb.psum("att_ps_av", [65, qtile]) for _ in range(NG)]
    kT = kb.sb("att_kT", [128, NC_D * s], BF16)
    vaug = kb.sb("att_vaug", [128, nkt * (NH * 65)], BF16)
    qt_sb = [kb.sb("att_qt", [128, NC_D * qtile], BF16) for _ in range(2)]
    P3 = [kb.sb("att_P", [128, qtile], BF16) for _ in range(3)]

    if kT_ranked:
        for r_ in range(2):
            kb.dma("sync",
                   kT[:].rearrange("p (r c t) -> p r c t", r=2, c=NC_D)
                   [:, r_:r_ + 1].squeeze(1),
                   kT_dram.ap().rearrange("(r c p) t -> r p c t", r=2, p=128)
                   [r_:r_ + 1].squeeze(0))
    else:
        kb.dma("sync", kT[:].rearrange("p (c s) -> p c s", c=NC_D),
               kT_dram.ap().rearrange("(c p) s -> p c s", p=128))

    def kT_off(ch, kt):
        if not kT_ranked:
            return ch * s + kt * 128
        r_ = (kt * 128) // t
        return r_ * (NC_D * t) + ch * t + (kt * 128 - r_ * t)
    kb.op("vector", lambda e: e.memset(vaug[:], 1.0), inc="V")
    kb.wait("vector", "V")
    vstg = [kb.sb("att_vstg", [128, D], BF16) for _ in range(2)]
    for kt in range(nkt):
        j = kt % 2
        if kt >= 2:
            kb.wait("sync", "Vc")
        ldv = kb.dma("sync", vstg[j][:, 0:D],
                     ktok_dram.ap()[kt * 128:(kt + 1) * 128, :], sem="qV%d" % j)
        kb.wait("vector", "qV%d" % j, ldv)
        kb.op("vector", lambda e, kt=kt, j=j: e.tensor_copy(
            vaug[:, kt * (NH * 65): (kt + 1) * (NH * 65)].rearrange(
                "p (h ee) -> p h ee", h=NH)[:, :, 0:64],
            vstg[j][:, 0:D].rearrange("p (h d) -> p h d", h=NH)), inc="Vc")
    kb.wait("tensor", "Vc")

    if decay is not None:
        lam = decay["lam"]
        wrow = kb.sb("att_wrow", [1, 2 * t], BF16)
        ub = kb.sb("att_ub", [128, 2 * nkt], F32)
        dec_tiles = kb.sb("att_dec", [128, nkt * qtile], BF16)
        dec_tmp = kb.sb("att_dectmp", [128, qtile], F32)
        ps_dec = kb.psum("att_ps_dec", [128, qtile])
        scr = dec_tiles[0:1, 0:2 * t].bitcast(F32)
        kb.dma("sync", scr[0:1, 0:t], decay["qpos_dram"].ap(), sem="qU")
        kb.wait("scalar", "qU")
        kb.op("scalar", lambda e: e.activation(wrow[0:1, 0:t], scr[0:1, 0:t],
                                               AF.Exp, scale=lam))
        kb.op("scalar", lambda e: e.activation(wrow[0:1, t:2 * t], scr[0:1, 0:t],
                                               AF.Exp, scale=-lam), inc="S")
        pcol = kb.sb("att_pcol", [128, 3], F32)
        kb.op("gpsimd", lambda e: e.iota(pcol[:, 0:1], pattern=[[1, 1]], base=0,
                                         channel_multiplier=1,
                                         allow_small_or_imprecise_dtypes=True),
              inc="G")
        kb.wait("scalar", "G")
        kb.op("scalar", lambda e: e.activation(pcol[:, 1:2], pcol[:, 0:1],
                                               AF.Exp, scale=-lam))
        kb.op("scalar", lambda e: e.activation(pcol[:, 2:3], pcol[:, 0:1],
                                               AF.Exp, scale=lam), inc="S")
        kb.wait("vector", "S")
        import math as _math
        for kc in range(nkt):
            kb.op("vector", lambda e, kc=kc: e.tensor_scalar_mul(
                ub[:, kc:kc + 1], pcol[:, 1:2],
                float(_math.exp(-lam * 128.0 * kc))))
            kb.op("vector", lambda e, kc=kc: e.tensor_scalar_mul(
                ub[:, nkt + kc:nkt + kc + 1], pcol[:, 2:3],
                float(_math.exp(lam * 128.0 * kc))), inc="V")
        kb.wait("vector", "qU")
        kb.wait("tensor", "V")


    dvc = {"n": 0}

    def build_decay_for_qt(qt):
        kb.wait("tensor", "Dv", dvc["n"])
        kb.op("tensor", lambda e, qt=qt: e.matmul(
            ps_dec[:, 0:qtile], ones_b1[0:1, :],
            wrow[0:1, qt * qtile:(qt + 1) * qtile],
            start=True, stop=True))
        kb.op("tensor", lambda e, qt=qt: e.matmul(
            ps_bc[:, 0:qtile], ones_b1[0:1, :],
            wrow[0:1, t + qt * qtile: t + (qt + 1) * qtile],
            start=True, stop=True), inc="T")
        tv = kb.semc["T"]
        kb.wait("vector", "T", tv)
        for kt in range(nkt):
            kb.wait("vector", "Dv")
            kb.seq("vector",
                   lambda e, kt=kt: e.tensor_scalar_mul(
                       dec_tmp[:, 0:qtile], ps_dec[:, 0:qtile], ub[:, kt:kt + 1]),
                   lambda e, kt=kt: e.tensor_scalar(
                       dec_tiles[:, kt * qtile:(kt + 1) * qtile],
                       ps_bc[:, 0:qtile], ub[:, nkt + kt:nkt + kt + 1], None,
                       ALU.mult),
                   lambda e, kt=kt: e.tensor_tensor(
                       dec_tiles[:, kt * qtile:(kt + 1) * qtile],
                       dec_tiles[:, kt * qtile:(kt + 1) * qtile],
                       dec_tmp[:, 0:qtile], ALU.min),
                   final_inc="Dv")
        dvc["n"] = kb.semc["Dv"]

    kb.wait("tensor", "qS")
    kb.wait("scalar", "qS")
    kb.wait("vector", "qS")

    navq = {}
    st_done = {}
    exp_done = {}
    av_done = {}
    qt_last_tv = {}
    qr_last = {"v": 0}
    avq = []

    def emit_av():
        i2, h2, hl2, kt2 = avq.pop(0)
        if kt2 == 0:
            kb.wait("tensor", "E2", navq.get(hl2, 0))
        kb.wait("tensor", "Ex", exp_done[i2])
        kb.op("tensor", lambda e, hl2=hl2, i2=i2, kt2=kt2, h2=h2: e.matmul(
            ps_av[hl2][0:65, 0:qtile],
            vaug[:, kt2 * (NH * 65) + h2 * 65: kt2 * (NH * 65) + h2 * 65 + 65],
            P3[i2 % 3][:, 0:qtile],
            start=(kt2 == 0), stop=(kt2 == nkt - 1)), inc="A")
        av_done[i2] = kb.semc["A"]
        if kt2 == nkt - 1:
            qt2 = i2 // (NH * nkt)
            g2 = (i2 // (NG * nkt)) % (NH // NG)
            ch2, po2 = h2 // 2, (h2 % 2) * 64
            kb.wait("vector", "A", av_done[i2])
            if hl2 == 0:
                kb.wait("vector", "qR", qr_last["v"])
            kb.op("vector", lambda e, hl2=hl2, ch2=ch2, po2=po2, qt2=qt2:
                  e.tensor_copy(
                av_T[po2:po2 + 64,
                     ch2 * t + qt2 * qtile: ch2 * t + (qt2 + 1) * qtile],
                ps_av[hl2][0:64, 0:qtile]))
            kb.op("vector", lambda e, hl2=hl2: e.tensor_copy(
                rs_row[0:1, hl2 * qtile:(hl2 + 1) * qtile],
                ps_av[hl2][64:65, 0:qtile]), inc="E2")
            navq[hl2] = kb.semc["E2"]
            if hl2 == NG - 1:
                kb.wait("sync", "E2", kb.semc["E2"])
                kb.dma("sync",
                       rs_d.ap()[g2 * NG:(g2 + 1) * NG,
                                 qt2 * qtile:(qt2 + 1) * qtile].unsqueeze(0),
                       rs_row[0:1, :].rearrange("p (h j) -> p h j", h=NG),
                       sem="qR")
                qr_last["v"] = kb.semc["qR"]

    i = 0
    for qt in range(nqt):
        qsb = qt_sb[qt % 2]
        if qt >= 2:
            kb.wait("sync", "T", qt_last_tv[qt - 2])
        dv = kb.dma("sync", qsb[:].rearrange("p (c j) -> p c j", c=NC_D),
                    q_dram.ap().rearrange("(c p) t -> p c t", p=128)
                    [:, :, qt * qtile:(qt + 1) * qtile], sem="qQ%d" % (qt % 2))
        if decay is not None:
            build_decay_for_qt(qt)
        kb.wait("tensor", "qQ%d" % (qt % 2), dv)
        for g in range(NH // NG):
            for hl in range(NG):
                h = g * NG + hl
                ch, po = h // 2, (h % 2) * 64
                for kt in range(nkt):
                    j = i % 2
                    if i >= 2:
                        _sm2, _v2 = st_done[i - 2]
                        kb.wait("tensor", _sm2, _v2)
                    _ko = kT_off(ch, kt)
                    kb.op("tensor", lambda e, j=j, po=po, _ko=_ko, qsb=qsb,
                          ch=ch: e.matmul(
                        ps_s[j][:, 0:qtile],
                        kT[po:po + 64, _ko:_ko + 128],
                        qsb[po:po + 64, ch * qtile:(ch + 1) * qtile],
                        start=True, stop=True), inc="T")
                    tv = kb.semc["T"]
                    qt_last_tv[qt] = tv
                    if decay is not None:
                        kb.wait("vector", "T", tv)
                        if i >= 3:
                            kb.wait("vector", "A", av_done[i - 3])
                        kb.op("vector", lambda e, j=j, kt=kt, i=i:
                              e.tensor_tensor(
                            P3[i % 3][:, 0:qtile], ps_s[j][:, 0:qtile],
                            dec_tiles[:, kt * qtile:(kt + 1) * qtile],
                            ALU.mult), inc="V")
                        st_done[i] = ("V", kb.semc["V"])
                        kb.wait("scalar", "V", kb.semc["V"])
                        kb.op("scalar", lambda e, i=i: e.activation(
                            P3[i % 3][:, 0:qtile], P3[i % 3][:, 0:qtile],
                            AF.Exp, scale=exp_scale), inc="Ex")
                    else:
                        kb.wait("scalar", "T", tv)
                        if i >= 3:
                            kb.wait("scalar", "A", av_done[i - 3])
                        kb.op("scalar", lambda e, j=j, i=i: e.activation(
                            P3[i % 3][:, 0:qtile], ps_s[j][:, 0:qtile],
                            AF.Exp, scale=exp_scale), inc="Ex")
                        st_done[i] = ("Ex", kb.semc["Ex"])
                    exp_done[i] = kb.semc["Ex"]
                    avq.append((i, h, hl, kt))
                    if len(avq) > 2:
                        emit_av()
                    i += 1
    while avq:
        emit_av()
    kb.barrier()
    kb.pop_scope()

    kb.push_scope()
    sel_sb = kb.sb("att_sel", [16, NC_D * 128], F32)
    recip16 = kb.sb("att_recip", [16, t], F32)
    xt_sb = [kb.sb("att_xt", [128, 512], F32) for _ in range(2)]
    ot_sb = [kb.sb("att_ot", [128, 512], F32) for _ in range(2)]
    tmp_t = kb.sb("att_tmp", [128, 512], F32)
    kb.dma("sync", sel_sb[:], sel_dram.ap(), sem="qR")
    kb.dma("sync", recip16[:, 0:t], rs_d.ap(), sem="qR")
    kb.wait("tensor", "qR")
    kb.wait("vector", "qR")
    kb.op("vector", lambda e: e.reciprocal(recip16[:, 0:t], recip16[:, 0:t]),
          inc="V")
    v_rec = kb.semc["V"]
    xv_d = {}

    def final_tile(dc, tt, av_src, add_x, out_to):
        i = dc * nxt + tt
        xi = i % 2
        if add_x:
            if i >= 2:
                kb.wait("sync", "qXo")
            kb.dma("sync", xt_sb[xi][:, 0:512],
                   x_dram.ap().rearrange("(c p) t -> c p t", p=128)
                   [dc:dc + 1, :, tt * 512:(tt + 1) * 512].squeeze(0),
                   sem="qX%d" % xi)
            xv = kb.semc["qX%d" % xi]
        kb.wait("tensor", "E", kb.semc["E"])
        kb.op("tensor", lambda e, dc=dc, tt=tt: e.matmul(
            ps_bc[:, 0:512], sel_sb[:, dc * 128:(dc + 1) * 128],
            recip16[:, tt * 512:(tt + 1) * 512], start=True, stop=True), inc="T")
        tv = kb.semc["T"]
        kb.wait("vector", "T", tv)
        if add_x:
            kb.wait("vector", "qX%d" % xi, xv)
            kb.wait("vector", "E")
            kb.seq("vector",
                   lambda e, dc=dc, tt=tt: e.tensor_tensor(
                       tmp_t[:, 0:512],
                       av_src[:, dc * t + tt * 512: dc * t + (tt + 1) * 512],
                       ps_bc[:, 0:512], ALU.mult),
                   lambda e, xi=xi: e.tensor_add(
                       ot_sb[xi][:, 0:512], xt_sb[xi][:, 0:512], tmp_t[:, 0:512]),
                   final_inc="E")
            kb.wait("sync", "E")
            xv_d[i] = kb.dma(
                "sync", out_to.ap().rearrange("(c p) t -> c p t", p=128)
                [dc:dc + 1, :, tt * 512:(tt + 1) * 512].squeeze(0),
                ot_sb[xi][:, 0:512], sem="qXo")
        else:
            kb.op("vector", lambda e, dc=dc, tt=tt: e.tensor_tensor(
                av_src[:, dc * t + tt * 512: dc * t + (tt + 1) * 512],
                av_src[:, dc * t + tt * 512: dc * t + (tt + 1) * 512],
                ps_bc[:, 0:512], ALU.mult), inc="E")

    if wo_dram is None:
        kb.wait("tensor", "V", v_rec)
        for dc in range(NC_D):
            for tt in range(nxt):
                final_tile(dc, tt, av_T, True, xout_dram)
    else:
        kb.wait("tensor", "V", v_rec)
        for dc in range(NC_D):
            for tt in range(nxt):
                final_tile(dc, tt, av_T, False, None)
        avn_done = kb.semc["E"]
        wsb, wsem, wv = load_weight_bf16(kb, wo_dram, "att_wo", D, D)
        kb.wait("tensor", wsem, wv)
        kb.wait("tensor", "E", avn_done)
        ps_o = [kb.psum("att_ps_o", [128, 512]) for _ in range(2)]
        xd = {}

        def ep(kb2, ps, mc, tt, tv):
            i = mc * nxt + tt
            xi = i % 2
            if i >= 2:
                kb.wait("sync", "qXo")
            kb.dma("sync", xt_sb[xi][:, 0:512],
                   x_dram.ap().rearrange("(c p) t -> c p t", p=128)
                   [mc:mc + 1, :, tt * 512:(tt + 1) * 512].squeeze(0),
                   sem="qX%d" % xi)
            xv = kb.semc["qX%d" % xi]
            kb.wait("vector", "T", tv)
            kb.wait("vector", "qX%d" % xi, xv)
            kb.op("vector", lambda e, ps=ps, xi=xi: e.tensor_add(
                ot_sb[xi][:, 0:512], xt_sb[xi][:, 0:512], ps[:, 0:512]), inc="E")
            kb.wait("sync", "E", kb.semc["E"])
            xd[i] = kb.dma(
                "sync", xout_dram.ap().rearrange("(c p) t -> c p t", p=128)
                [mc:mc + 1, :, tt * 512:(tt + 1) * 512].squeeze(0),
                ot_sb[xi][:, 0:512], sem="qXo")
        emit_matmul(kb, wsb, D, D, av_T, t, ep, ps_o)
    kb.wait("sync", "qXo")
    kb.wait("sync", "qDo")
    kb.barrier()
    kb.pop_scope()


def emit_transpose_fm_to_tok(kb, src_fm, dst_tok, t, d=D, dtype=BF16,
                             ident=None, ps_tp=None):
    ntc = t // 128
    for dc in range(d // 128):
        for tc in range(ntc):
            kb.op("tensor", lambda e, dc=dc, tc=tc: e.transpose(
                ps_tp[:, 0:128],
                src_fm[:, dc * t + tc * 128: dc * t + tc * 128 + 128],
                ident[:, 0:128]), inc="T")
            tv = kb.semc["T"]
            kb.wait("vector", "T", tv)
            kb.op("vector", lambda e, dc=dc, tc=tc: e.tensor_copy(
                dst_tok[:, tc * d + dc * 128: tc * d + dc * 128 + 128],
                ps_tp[:, 0:128]), inc="E")
            kb.wait("tensor", "E", kb.semc["E"])


def emit_transpose_to_dram(kb, src_fm, out_dram, t, d=D, ident=None, ps_tp=None):
    ntc = t // 128
    stg = [kb.sb("tp_stg", [128, d], BF16) for _ in range(2)]
    for tc in range(ntc):
        j = tc % 2
        if tc >= 2:
            kb.wait("vector", "qTp")
        for dc in range(d // 128):
            kb.op("tensor", lambda e, dc=dc, tc=tc: e.transpose(
                ps_tp[:, 0:128],
                src_fm[:, dc * t + tc * 128: dc * t + tc * 128 + 128],
                ident[:, 0:128]), inc="T")
            tv = kb.semc["T"]
            kb.wait("vector", "T", tv)
            kb.op("vector", lambda e, dc=dc, j=j: e.tensor_copy(
                stg[j][:, dc * 128:(dc + 1) * 128], ps_tp[:, 0:128]), inc="E")
            kb.wait("tensor", "E", kb.semc["E"])
        kb.wait("sync", "E", kb.semc["E"])
        kb.dma("sync", out_dram.ap()[tc * 128:(tc + 1) * 128, :],
               stg[j][:, 0:d], sem="qTp")
    kb.wait("sync", "qTp")


def make_dram_epilogue(kb, out_dram, t, eng="scalar", n_tile=512, bias_sb=None,
                       scale=1.0, func=None, dtype=BF16, mw=128, q="sync"):
    f = func or AF.Identity
    tiles = [kb.sb("ep_dram", [128, n_tile], dtype) for _ in range(2)]
    state = {"i": 0, "dma_done": {}}
    oap = out_dram.ap()

    def ep(kb, ps, mc, tt, tv):
        i = state["i"]
        tile = tiles[i % 2]
        if i >= 2:
            kb.wait(eng, "qDo")
        kb.wait(eng, "T", tv)
        if bias_sb is not None:
            kb.op(eng, lambda e, ps=ps, mc=mc, tile=tile: e.activation(
                tile[0:mw, 0:n_tile], ps[0:mw, 0:n_tile], f,
                bias=bias_sb[:, mc:mc + 1], scale=scale), inc="E")
        elif scale != 1.0 or func is not None:
            kb.op(eng, lambda e, ps=ps, tile=tile: e.activation(
                tile[0:mw, 0:n_tile], ps[0:mw, 0:n_tile], f, scale=scale), inc="E")
        else:
            kb.op(eng, lambda e, ps=ps, tile=tile: (
                e.copy(tile[0:mw, 0:n_tile], ps[0:mw, 0:n_tile])
                if eng == "scalar" else
                e.tensor_copy(tile[0:mw, 0:n_tile], ps[0:mw, 0:n_tile])), inc="E")
        ev = kb.semc["E"]
        kb.wait(q, "E", ev)
        state["dma_done"][i % 2] = kb.dma(
            q, oap[mc * mw:(mc + 1) * mw, tt * n_tile:(tt + 1) * n_tile],
            tile[0:mw, 0:n_tile], sem="qDo")
        state["i"] += 1
    return ep


def emit_proj_to_dram(kb, w_dram, kin, mout, rhs_sb, t, out_dram, ps_list,
                      eng="scalar", bias_sb=None, scale=1.0, func=None,
                      dtype=BF16, wname="w"):
    wsb, wsem, wv = load_weight_bf16(kb, w_dram, wname, kin, mout)
    kb.wait("tensor", wsem, wv)
    ep = make_dram_epilogue(kb, out_dram, t, eng=eng, bias_sb=bias_sb,
                            scale=scale, func=func, dtype=dtype)
    emit_matmul(kb, wsb, kin, mout, rhs_sb, t, ep, ps_list)
    return wsb


def load_bias_cols(kb, b_dram, name, n=D):
    bsb = kb.sb(name, [128, n // 128], F32)
    kb.dma("sync", bsb[:], b_dram.ap().rearrange("(c p) -> p c", p=128),
           allow_slow_non_contiguous=True)
    return bsb




"""v0 launch builders (one program per phase) + host glue.

Each build_* returns a fresh nc plus the list of input names it expects.
Host glue slices/concats between launches (layout only).
"""
import ml_dtypes


BF = ml_dtypes.bfloat16


def sel_matrix():
    m = np.zeros((16, 8 * 128), dtype=np.float32)
    for h in range(16):
        dc, po = h // 2, (h % 2) * 64
        m[h, dc * 128 + po: dc * 128 + po + 64] = 1.0
    return m


def build_prep_a(kb=None):
    _standalone = kb is None
    if _standalone:
        kb = KB()
    x = kb.inp("x", [D, T])
    wq = kb.inp("wq", [D, D])
    wk = kb.inp("wk", [D, D])
    bq = kb.inp("bq", [D])
    bk = kb.inp("bk", [D])
    ident = kb.inp("ident", [128, 128], BF16)
    o_q = kb.outp("q_T", [D, T], BF16)
    o_k = kb.outp("k_T", [D, T], BF16)
    o_kt = kb.outp("k_tok", [T, D], BF16)

    x_sb = kb.sb("x_sb", [128, NC_D * T], F32)
    xhat = kb.sb("xhat", [128, NC_D * T], BF16)
    aux = alloc_ln_aux(kb, sq=xhat)
    id_sb = kb.sb("id_sb", [128, 128], BF16)
    k_sb = aux["xb"]

    load_x(kb, x, x_sb)
    kb.dma("sync", id_sb[:], ident.ap())
    bq_sb = load_bias_cols(kb, bq, "bq_sb")
    bk_sb = load_bias_cols(kb, bk, "bk_sb")
    kb.wait("vector", "qS")
    kb.wait("scalar", "qS")
    emit_ln(kb, x_sb, xhat, aux)
    kb.wait("tensor", "V")
    scale = float(D) ** -0.5
    bqs = kb.sb("bqs", [128, NC_D], F32)
    kb.op("vector", lambda e: e.tensor_scalar_mul(bqs[:], bq_sb[:], scale), inc="V")
    kb.wait("scalar", "V")
    ps = [kb.psum("pp", [128, 512]) for _ in range(2)]
    w_sb = kb.sb("w_sb", [128, NC_D * D], BF16)
    sm, v = load_weight_into(kb, w_sb, wq, D, D)
    kb.wait("tensor", sm, v)
    ep_q = make_dram_epilogue(kb, o_q, T, eng="scalar", bias_sb=bqs, scale=scale)
    emit_matmul(kb, w_sb, D, D, xhat, T, ep_q, ps)
    kb.wait("gpsimd", "T")
    sm, v = load_weight_into(kb, w_sb, wk, D, D)
    kb.wait("tensor", sm, v)
    ep_k = make_copy_epilogue(k_sb, T, eng="scalar", bias_sb=bk_sb)
    emit_matmul(kb, w_sb, D, D, xhat, T, ep_k, ps)
    kb.wait("sync", "E")
    kb.dma("sync", o_k.ap().rearrange("(c p) t -> p c t", p=128),
           k_sb[:].rearrange("p (c t) -> p c t", c=NC_D), sem="qDo")
    kb.wait("tensor", "E")
    ps_tp = kb.psum("ps_tp", [128, 128], BF16)
    emit_transpose_to_dram(kb, k_sb, o_kt, T, d=D, ident=id_sb, ps_tp=ps_tp)
    kb.wait("sync", "qDo")
    if _standalone:
        return kb.build()


def build_attn_a(ranked=False, kb=None):
    _standalone = kb is None
    if _standalone:
        kb = KB()
    x = kb.inp("x", [D, T])
    q = kb.inp("q_T", [D, T], BF16)
    kT = kb.inp("kT_full", [D, S], BF16)
    ktok = kb.inp("ktok_full", [S, D], BF16)
    sel = kb.inp("sel", [16, D])
    xo = kb.outp("x_out", [D, T])
    emit_attention(kb, q_dram=q, kT_dram=kT, ktok_dram=ktok, x_dram=x,
                   xout_dram=xo, sel_dram=sel, qtile=512, kT_ranked=ranked)
    if _standalone:
        return kb.build()



def np_ln(x):
    m = x.mean(-1, keepdims=True)
    v = x.var(-1, keepdims=True)
    return (x - m) / np.sqrt(v + 1e-5)


def np_attn_a(x_full, Wq, bq, Wk, bk):
    xh = np_ln(x_full)
    B, S_, C = x_full.shape
    H, Dh = 16, 64
    q = (xh @ Wq + bq).reshape(B, S_, H, Dh).transpose(0, 2, 1, 3)
    k = (xh @ Wk + bk).reshape(B, S_, H, Dh).transpose(0, 2, 1, 3)
    att = np.einsum("bhnd,bhmd->bhnm", q, k) * (C ** -0.5)
    att = att - att.max(-1, keepdims=True)
    att = np.exp(att)
    att = att / att.sum(-1, keepdims=True)
    out = np.einsum("bhnm,bhmd->bhnd", att, k)
    return out.transpose(0, 2, 1, 3).reshape(B, S_, C)


def run8(nc, in_maps):
    from concourse.bass_utils import run_bass_kernel_spmd
    return run_bass_kernel_spmd(nc, in_maps, core_ids=list(range(8))).results




def emit_ternarize(kb, w_dram, wq_dram, rows, cols):
    nch = rows // 128
    kb.push_scope()
    wstream = [kb.sb("tern_w", [128, cols], F32) for _ in range(2)]
    acc = kb.sb("tern_acc", [128, 1], F32)
    red = kb.sb("tern_red", [128, 1], F32)
    onec = kb.sb("tern_one", [128, 1], F32)
    ones_row = kb.sb("tern_or", [1, 128], F32)
    sc = kb.sb("tern_sc", [1, 1], F32)
    invc = kb.sb("tern_inv", [128, 1], F32)
    a_t = kb.sb("tern_abs", [128, cols], F32)
    s_t = kb.sb("tern_sgn", [128, cols], BF16)
    m_t = kb.sb("tern_msk", [128, cols], BF16)
    o_t = [kb.sb("tern_out", [128, cols], BF16) for _ in range(2)]
    ps1 = kb.psum("tern_ps", [1, 1])
    psb = kb.psum("tern_psb", [128, 1])
    kb.op("vector", lambda e: e.memset(acc[:], 0.0))
    kb.op("vector", lambda e: e.memset(onec[:], 1.0))
    kb.op("vector", lambda e: e.memset(ones_row[:], 1.0), inc="V")
    kb.wait("vector", "V")
    for c in range(nch):
        j = c % 2
        dv = kb.dma("sync", wstream[j][:], w_dram.ap()[c * 128:(c + 1) * 128, :],
                    sem="qW%d" % j)
        kb.wait("vector", "qW%d" % j, dv)
        kb.seq("vector",
               lambda e, j=j: e.reduce_sum(red[:], wstream[j][:], axis=AX.X,
                                           apply_absolute_value=True),
               lambda e: e.tensor_add(acc[:], acc[:], red[:]), final_inc="Vt")
        if c + 2 < nch:
            pass
        kb.wait("sync", "Vt")
    kb.wait("tensor", "Vt")
    kb.op("tensor", lambda e: e.matmul(ps1[0:1, 0:1], acc[:], onec[:],
                                       start=True, stop=True), inc="T")
    kb.wait("vector", "T")
    kb.seq("vector",
           lambda e: e.tensor_scalar(sc[:], ps1[0:1, 0:1], 1.0 / (rows * cols),
                                     1e-8, ALU.mult, ALU.max),
           lambda e: e.reciprocal(sc[:], sc[:]), final_inc="V")
    kb.wait("tensor", "V")
    kb.op("tensor", lambda e: e.matmul(psb[:, 0:1], ones_row[0:1, :],
                                       sc[0:1, 0:1], start=True, stop=True),
          inc="T")
    kb.wait("vector", "T")
    kb.op("vector", lambda e: e.tensor_copy(invc[:], psb[:, 0:1]), inc="V")
    kb.wait("scalar", "V")
    kb.wait("vector", "V")
    for c in range(nch):
        j = c % 2
        dv = kb.dma("sync", wstream[j][:], w_dram.ap()[c * 128:(c + 1) * 128, :],
                    sem="qW%d" % j)
        kb.wait("scalar", "qW%d" % j, dv)
        kb.wait("scalar", "Vt")
        kb.seq("scalar",
               lambda e, j=j: e.activation(a_t[:], wstream[j][:], AF.Abs),
               lambda e, j=j: e.activation(s_t[:], wstream[j][:], AF.Sign),
               final_inc="S")
        kb.wait("vector", "S")
        kb.wait("vector", "qTo")
        kb.seq("vector",
               lambda e: e.tensor_scalar(m_t[:], a_t[:], invc[:], 0.5,
                                         ALU.mult, ALU.is_gt),
               lambda e, j=j: e.tensor_mul(o_t[j][:], m_t[:], s_t[:]),
               final_inc="Vt")
        kb.wait("sync", "Vt")
        kb.dma("sync", wq_dram.ap()[c * 128:(c + 1) * 128, :], o_t[j][:],
               sem="qTo")
    kb.wait("sync", "qTo")
    kb.barrier()
    kb.pop_scope()


def emit_matmul_stream(kb, w_dram, kin, mout, rhs_sb, rhs_t, cols0, t,
                       epilogue, ps_list, wq="sync"):
    nkc = kin // 128
    nmc = mout // 128
    nt = t // 512
    wtiles = [kb.sb("mmsw", [128, nkc * 128], BF16) for _ in range(2)]
    pending = {}
    i = 0
    for mc in range(nmc):
        wj = mc % 2
        if mc >= 2:
            kb.wait("tensor" if wq == "sync" else "tensor", "qMS%d" % wj)
        kb.wait(wq, "T")
        wv = kb.dma(wq, wtiles[wj][:].rearrange("p (kc m) -> p kc m", kc=nkc),
                    w_dram.ap().rearrange("(kc p) m -> p kc m", p=128)
                    [:, :, mc * 128:(mc + 1) * 128], sem="qMS%d" % wj)
        kb.wait("tensor", "qMS%d" % wj, wv)
        for tt in range(nt):
            ps = ps_list[i % len(ps_list)]
            if (i % len(ps_list)) in pending:
                kb.wait("tensor", "E", pending[i % len(ps_list)])
            else:
                kb.wait("tensor", "E")
            for kc in range(nkc):
                kb.op("tensor", lambda e, ps=ps, kc=kc, wj=wj, tt=tt: e.matmul(
                    ps[:, 0:512],
                    wtiles[wj][:, kc * 128:(kc + 1) * 128],
                    rhs_sb[:, kc * rhs_t + cols0 + tt * 512:
                           kc * rhs_t + cols0 + (tt + 1) * 512],
                    start=(kc == 0), stop=(kc == nkc - 1)),
                    inc=("T" if kc == nkc - 1 else None))
            tv = kb.semc["T"]
            epilogue(kb, ps, mc, tt, tv)
            pending[i % len(ps_list)] = kb.semc["E"]
            i += 1


def emit_ffn0(kb, x_dram, w1q_dram, b1_dram, w2q_dram, b2_dram, xout_dram):
    FF = 4096
    x_sb = kb.sb("ffn_x", [128, NC_D * T], F32)
    xhat = kb.sb("ffn_xh", [128, NC_D * T], BF16)
    load_x(kb, x_dram, x_sb)
    b1_sb = load_bias_cols(kb, b1_dram, "b1_sb", FF)
    b2_sb = load_bias_cols(kb, b2_dram, "b2_sb", D)
    kb.push_scope()
    aux = alloc_ln_aux(kb, sq=xhat)
    kb.wait("vector", "qS")
    kb.wait("scalar", "qS")
    emit_ln(kb, x_sb, xhat, aux)
    kb.barrier()
    kb.pop_scope()
    kb.push_scope()
    h_half = kb.sb("ffn_h", [128, (FF // 128) * 1024], BF16)
    ps = [kb.psum("ffn_ps", [128, 512]) for _ in range(2)]
    ot = [kb.sb("ffn_ot", [128, 512], F32) for _ in range(2)]
    for th in range(2):
        cols0 = th * 1024
        def ep_h(kb2, psv, mc, tt, tv, th=th):
            kb.wait("scalar", "T", tv)
            kb.op("scalar", lambda e, psv=psv, mc=mc, tt=tt: e.activation(
                h_half[:, mc * 1024 + tt * 512: mc * 1024 + (tt + 1) * 512],
                psv[:, 0:512], AF.Gelu, bias=b1_sb[:, mc:mc + 1]), inc="E")
        emit_matmul_stream(kb, w1q_dram, D, FF, xhat, T, cols0, 1024, ep_h, ps)
        xo_i = {"i": 0}

        def ep_o(kb2, psv, mc, tt, tv, th=th):
            i = xo_i["i"]
            xi = i % 2
            kb.wait("vector", "T", tv)
            if i >= 2:
                kb.wait("vector", "qFo")
            kb.op("vector", lambda e, psv=psv, mc=mc, tt=tt, xi=xi:
                  e.scalar_tensor_tensor(
                      ot[xi][:, 0:512], psv[:, 0:512], b2_sb[:, mc:mc + 1],
                      x_sb[:, mc * T + th * 1024 + tt * 512:
                           mc * T + th * 1024 + (tt + 1) * 512],
                      ALU.add, ALU.add), inc="E")
            kb.wait("sync", "E")
            kb.dma("sync", xout_dram.ap().rearrange("(c p) t -> c p t", p=128)
                   [mc:mc + 1, :, th * 1024 + tt * 512:
                    th * 1024 + (tt + 1) * 512].squeeze(0),
                   ot[xi][:, 0:512], sem="qFo")
            xo_i["i"] += 1
        emit_matmul_stream(kb, w2q_dram, FF, D, h_half, 1024, 0, 1024, ep_o, ps)
        kb.barrier()
    kb.wait("sync", "qFo")
    kb.barrier()
    kb.pop_scope()


def build_ffn0(ternarize, kb=None):
    _standalone = kb is None
    if _standalone:
        kb = KB()
    x = kb.inp("x", [D, T])
    xo = kb.outp("x_out", [D, T])
    b1 = kb.inp("b1", [4096])
    b2 = kb.inp("b2", [D])
    if ternarize:
        w1t = kb.inp("w1t", [D, 4096])
        w2t = kb.inp("w2t", [4096, D])
        w1q = kb.outp("w1q", [D, 4096], BF16)
        w2q = kb.outp("w2q", [4096, D], BF16)
        emit_ternarize(kb, w1t, w1q, D, 4096)
        emit_ternarize(kb, w2t, w2q, 4096, D)
    else:
        w1q = kb.inp("w1q", [D, 4096], BF16)
        w2q = kb.inp("w2q", [4096, D], BF16)
    emit_ffn0(kb, x, w1q, b1, w2q, b2, xo)
    if _standalone:
        return kb.build()


def np_ternary(w):
    s = max(np.abs(w).mean(), 1e-8)
    return np.clip(np.round(w / s), -1, 1)


def np_ffn0(x_full, W1, b1, W2, b2):
    from scipy.special import erf
    xh = np_ln(x_full)
    w1t = np_ternary(W1)
    w2t = np_ternary(W2)
    h = np.einsum("bsd,fd->bsf", xh, w1t) + b1
    h = 0.5 * h * (1.0 + erf(h / np.sqrt(2.0)))
    return np.einsum("bsf,df->bsd", h, w2t) + b2



def emit_matmul_tok(kb, lhs_sb, kin, w_sb, mout, t, epilogue, ps_list):
    nkc = kin // 128
    ntc = t // 128
    nnt = mout // 512
    pending = {}
    i = 0
    for tc in range(ntc):
        for nt in range(nnt):
            ps = ps_list[i % len(ps_list)]
            if (i % len(ps_list)) in pending:
                kb.wait("tensor", "E", pending[i % len(ps_list)])
            else:
                kb.wait("tensor", "E")
            for kc in range(nkc):
                kb.op("tensor", lambda e, ps=ps, kc=kc, tc=tc, nt=nt: e.matmul(
                    ps[:, 0:512],
                    lhs_sb[:, kc * t + tc * 128: kc * t + tc * 128 + 128],
                    w_sb[:, kc * mout + nt * 512: kc * mout + (nt + 1) * 512],
                    start=(kc == 0), stop=(kc == nkc - 1)),
                    inc=("T" if kc == nkc - 1 else None))
            tv = kb.semc["T"]
            epilogue(kb, ps, tc, nt, tv)
            pending[i % len(ps_list)] = kb.semc["E"]
            i += 1


def build_prep_b(kb=None):
    _standalone = kb is None
    if _standalone:
        kb = KB()
    x = kb.inp("x", [D, T])
    wq = kb.inp("wq", [D, D])
    wk = kb.inp("wk", [D, D])
    wv = kb.inp("wv", [D, D])
    wd = kb.inp("wd", [D, 256])
    wuk = kb.inp("wuk", [256, D])
    wuv = kb.inp("wuv", [256, D])
    o_q = kb.outp("q_T", [D, T], BF16)
    o_k = kb.outp("krec_T", [D, T], BF16)
    o_v = kb.outp("vrec_tok", [T, D], BF16)

    x_sb = kb.sb("pb_x", [128, NC_D * T], F32)
    xhat = kb.sb("pb_xh", [128, NC_D * T], BF16)
    load_x(kb, x, x_sb)
    kb.push_scope()
    aux = alloc_ln_aux(kb, sq=xhat)
    kb.wait("vector", "qS")
    kb.wait("scalar", "qS")
    emit_ln(kb, x_sb, xhat, aux)
    kb.barrier()
    kb.pop_scope()
    kb.push_scope()
    scale = 1.0 / (np.sqrt(float(D)) + 1e-8)
    ps = [kb.psum("pb_ps", [128, 512]) for _ in range(2)]
    w_sb = kb.sb("pb_w", [128, NC_D * D], BF16)
    k_sb = kb.sb("pb_k", [128, NC_D * T], BF16)
    v_sb = kb.sb("pb_v", [128, NC_D * T], BF16)
    lat = kb.sb("pb_lat", [128, 2 * T], BF16)
    sm, v = load_weight_into(kb, w_sb, wq, D, D)
    kb.wait("tensor", sm, v)
    ep_q = make_dram_epilogue(kb, o_q, T, eng="scalar", scale=scale,
                              func=AF.Identity)
    emit_matmul(kb, w_sb, D, D, xhat, T, ep_q, ps)
    kb.wait("gpsimd", "T")
    sm, v = load_weight_into(kb, w_sb, wk, D, D)
    kb.wait("tensor", sm, v)
    emit_matmul(kb, w_sb, D, D, xhat, T,
                make_copy_epilogue(k_sb, T, eng="scalar"), ps)
    kb.wait("gpsimd", "T")
    sm, v = load_weight_into(kb, w_sb, wv, D, D)
    kb.wait("tensor", sm, v)
    emit_matmul(kb, w_sb, D, D, xhat, T,
                make_copy_epilogue(v_sb, T, eng="scalar"), ps)
    kb.wait("vector", "E")
    for dc in range(NC_D):
        kb.op("vector", lambda e, dc=dc: e.tensor_add(
            k_sb[:, dc * T:(dc + 1) * T], k_sb[:, dc * T:(dc + 1) * T],
            v_sb[:, dc * T:(dc + 1) * T]), inc="V")
    kb.wait("tensor", "V")
    wd_sb = kb.sb("pb_wd", [128, NC_D * 256], BF16)
    sm, v = load_weight_into(kb, wd_sb, wd, D, 256)
    kb.wait("tensor", sm, v)
    emit_matmul(kb, wd_sb, D, 256, k_sb, T,
                make_copy_epilogue(lat, T, eng="scalar"), ps)
    wu_sb = kb.sb("pb_wu", [128, 2 * D], BF16)
    sm, v = load_weight_into(kb, wu_sb, wuk, 256, D)
    kb.wait("tensor", sm, v)
    kb.wait("tensor", "E")
    emit_matmul(kb, wu_sb, 256, D, lat, T,
                make_dram_epilogue(kb, o_k, T, eng="scalar"), ps)
    kb.wait("gpsimd", "T")
    sm, v = load_weight_into(kb, wu_sb, wuv, 256, D)
    kb.wait("tensor", sm, v)
    vtiles = [kb.sb("pb_vt", [128, 512], BF16) for _ in range(2)]
    vi = {"i": 0}

    def ep_vt(kb2, psv, tc, nt, tv):
        i = vi["i"]
        xi = i % 2
        kb.wait("scalar", "T", tv)
        if i >= 2:
            kb.wait("scalar", "qVo")
        kb.op("scalar", lambda e, psv=psv, xi=xi: e.copy(
            vtiles[xi][:, 0:512], psv[:, 0:512]), inc="E")
        kb.wait("sync", "E")
        kb.dma("sync", o_v.ap()[tc * 128:(tc + 1) * 128, nt * 512:(nt + 1) * 512],
               vtiles[xi][:, 0:512], sem="qVo")
        vi["i"] += 1
    emit_matmul_tok(kb, lat, 256, wu_sb, D, T, ep_vt, ps)
    kb.wait("sync", "qVo")
    kb.wait("sync", "qDo")
    kb.barrier()
    kb.pop_scope()
    if _standalone:
        return kb.build()


def build_attn_b(ranked=False, kb=None):
    _standalone = kb is None
    if _standalone:
        kb = KB()
    x = kb.inp("x", [D, T])
    q = kb.inp("q_T", [D, T], BF16)
    kT = kb.inp("kT_full", [D, S], BF16)
    ktok = kb.inp("ktok_full", [S, D], BF16)
    sel = kb.inp("sel", [16, D])
    wo = kb.inp("wo", [D, D])
    pos = kb.inp("pos", [1, S])
    qpos = kb.inp("qpos", [1, T])
    xo = kb.outp("x_out", [D, T])
    emit_attention(kb, q_dram=q, kT_dram=kT, ktok_dram=ktok, x_dram=x,
                   xout_dram=xo, sel_dram=sel, qtile=256,
                   decay=dict(lam=1e-4, pos_dram=pos, qpos_dram=qpos),
                   wo_dram=wo, kT_ranked=ranked)
    if _standalone:
        return kb.build()


def np_attn_b(x_full, Wq, Wk, Wv, Wo, Wd, Wuk, Wuv):
    xh = np_ln(x_full)
    B, S_, C = x_full.shape
    H, Dh = 16, 64
    q = xh @ Wq
    k = xh @ Wk
    v = xh @ Wv
    lat = (k + v) @ Wd
    kr = lat @ Wuk
    vr = lat @ Wuv

    def heads(tn):
        return tn.reshape(B, S_, H, Dh).transpose(0, 2, 1, 3)
    qh, kh, vh = heads(q), heads(kr), heads(vr)
    sc = np.einsum("bhqd,bhkd->bhqk", qh, kh) / (np.sqrt(C) + 1e-8)
    idx = np.arange(S_, dtype=np.float32)
    decay = np.exp(-1e-4 * np.abs(idx[:, None] - idx[None, :]))
    sc = sc * decay[None, None]
    sc = sc - sc.max(-1, keepdims=True)
    a = np.exp(sc)
    a = a / a.sum(-1, keepdims=True)
    out = np.einsum("bhqk,bhkd->bhqd", a, vh)
    return out.transpose(0, 2, 1, 3).reshape(B, S_, C) @ Wo



def build_prep_qkv(kb=None):
    _standalone = kb is None
    if _standalone:
        kb = KB()
    x = kb.inp("x", [D, T])
    wqkv = kb.inp("wqkvT", [D, D])
    ident = kb.inp("ident", [128, 128], BF16)
    o_q = kb.outp("qkv_T", [D, T], BF16)
    o_t = kb.outp("qkv_tok", [T, D], BF16)
    x_sb = kb.sb("pc_x", [128, NC_D * T], F32)
    xhat = kb.sb("pc_xh", [128, NC_D * T], BF16)
    id_sb = kb.sb("pc_id", [128, 128], BF16)
    load_x(kb, x, x_sb)
    kb.dma("sync", id_sb[:], ident.ap())
    kb.push_scope()
    aux = alloc_ln_aux(kb, sq=xhat)
    q_sb = aux["xb"]
    kb.wait("vector", "qS")
    kb.wait("scalar", "qS")
    emit_ln(kb, x_sb, xhat, aux)
    kb.wait("tensor", "V")
    ps = [kb.psum("pc_ps", [128, 512]) for _ in range(2)]
    w_sb = kb.sb("pc_w", [128, NC_D * D], BF16)
    sm, v = load_weight_into(kb, w_sb, wqkv, D, D)
    kb.wait("tensor", sm, v)
    emit_matmul(kb, w_sb, D, D, xhat, T,
                make_copy_epilogue(q_sb, T, eng="scalar"), ps)
    kb.wait("sync", "E")
    kb.dma("sync", o_q.ap().rearrange("(c p) t -> p c t", p=128),
           q_sb[:].rearrange("p (c t) -> p c t", c=NC_D), sem="qDo")
    kb.wait("tensor", "E")
    ps_tp = kb.psum("pc_tp", [128, 128], BF16)
    emit_transpose_to_dram(kb, q_sb, o_t, T, d=D, ident=id_sb, ps_tp=ps_tp)
    kb.wait("sync", "qDo")
    kb.barrier()
    kb.pop_scope()
    if _standalone:
        return kb.build()


def build_attn_c(ranked=False, kb=None):
    _standalone = kb is None
    if _standalone:
        kb = KB()
    x = kb.inp("x", [D, T])
    q = kb.inp("q_T", [D, T], BF16)
    kT = kb.inp("kT_full", [D, S], BF16)
    ktok = kb.inp("ktok_full", [S, D], BF16)
    sel = kb.inp("sel", [16, D])
    wo = kb.inp("wo", [D, D])
    xo = kb.outp("x_out", [D, T])
    emit_attention(kb, q_dram=q, kT_dram=kT, ktok_dram=ktok, x_dram=x,
                   xout_dram=xo, sel_dram=sel, qtile=512, wo_dram=wo,
                   exp_scale=0.125, kT_ranked=ranked)
    if _standalone:
        return kb.build()


def np_attn_c(x_full, W_qkv, W_o):
    xh = np_ln(x_full)
    B, S_, C = x_full.shape
    H, Dh = 16, 64
    qkv = (xh @ W_qkv.T).reshape(B, S_, H, Dh).transpose(0, 2, 1, 3)
    sc = np.einsum("bhif,bhjf->bhij", qkv, qkv) / np.sqrt(float(Dh))
    sc = sc - sc.max(-1, keepdims=True)
    a = np.exp(sc)
    a = a / a.sum(-1, keepdims=True)
    out = np.einsum("bhij,bhjf->bihf", a, qkv)
    return out.reshape(B, S_, C) @ W_o.T



def build_prep_ssm(kb=None):
    _standalone = kb is None
    if _standalone:
        kb = KB()
    x = kb.inp("x", [D, T])
    o = kb.outp("xhat", [D, T], BF16)
    x_sb = kb.sb("ps_x", [128, NC_D * T], F32)
    xhat = kb.sb("ps_xh", [128, NC_D * T], BF16)
    load_x(kb, x, x_sb)
    kb.push_scope()
    aux = alloc_ln_aux(kb, sq=xhat)
    kb.wait("vector", "qS")
    kb.wait("scalar", "qS")
    emit_ln(kb, x_sb, xhat, aux)
    kb.wait("sync", "V")
    kb.dma("sync", o.ap().rearrange("(c p) t -> p c t", p=128),
           xhat[:].rearrange("p (c t) -> p c t", c=NC_D), sem="qDo")
    kb.wait("sync", "qDo")
    kb.barrier()
    kb.pop_scope()
    if _standalone:
        return kb.build()


def build_ssm_main(ranked=False, kb=None):
    _standalone = kb is None
    if _standalone:
        kb = KB()
    xh_d = kb.inp("xhat_full", [2 * D, T] if ranked else [D, S], BF16)
    w_xp = kb.inp("w_xp", [D, 1024])
    w_res = kb.inp("w_res", [D, 1024])
    convw = kb.inp("convw", [1024, 4])
    convb = kb.inp("convb", [1024])
    wxT = kb.inp("wxT", [1024, 96])
    o_xc = kb.outp("xc", [1024, S], BF16)
    o_sres = kb.outp("sres", [1024, S], BF16)
    o_pp = kb.outp("projp", [96, S])

    SP3 = S + 3
    xp_sb = kb.sb("sm_xp", [128, NC_D * SP3], BF16)
    xc_sb = kb.sb("sm_xc", [128, NC_D * S], BF16)
    acc = kb.sb("sm_acc", [128, S], F32)
    cw = kb.sb("sm_cw", [128, 4 * NC_D], F32)
    kb.dma("sync", cw[:].rearrange("p (mc k) -> p mc k", mc=NC_D),
           convw.ap().rearrange("(mc p) k -> p mc k", p=128),
           allow_slow_non_contiguous=True)
    cb = load_bias_cols(kb, convb, "sm_cb", 1024)
    w1 = kb.sb("sm_w1", [128, NC_D * 1024], BF16)
    w2 = kb.sb("sm_w2", [128, NC_D * 1024], BF16)
    s1, v1 = load_weight_into(kb, w1, w_xp, D, 1024)
    s2, v2 = load_weight_into(kb, w2, w_res, D, 1024)
    ps = [kb.psum("sm_ps", [128, 512]) for _ in range(2)]
    rhs = [kb.sb("sm_rhs", [128, NC_D * 512], BF16) for _ in range(2)]
    for mc in range(NC_D):
        kb.op("vector", lambda e, mc=mc: e.memset(
            xp_sb[:, mc * SP3: mc * SP3 + 3], 0.0), inc="V")
    kb.wait("tensor", s1, v1)
    kb.wait("tensor", s2, v2)
    kb.wait("tensor", "V")
    ep_sres = make_dram_epilogue(kb, o_sres, S, eng="scalar", func=AF.Silu)
    pend = {}
    for tt in range(S // 512):
        j = tt % 2
        if ranked:
            r_i, toff = tt // 4, (tt % 4) * 512
            rv = kb.dma("sync", rhs[j][:].rearrange("p (c n) -> p c n", c=NC_D),
                        xh_d.ap().rearrange("(r c p) t -> r p c t", p=128, r=2)
                        [r_i:r_i + 1, :, :, toff:toff + 512].squeeze(0),
                        sem="qR%d" % j)
        else:
            rv = kb.dma("sync", rhs[j][:].rearrange("p (c n) -> p c n", c=NC_D),
                        xh_d.ap().rearrange("(c p) s -> p c s", p=128)
                        [:, :, tt * 512:(tt + 1) * 512], sem="qR%d" % j)
        kb.wait("tensor", "qR%d" % j, rv)
        for mc in range(NC_D):
            psv = ps[mc % 2]
            if (mc % 2) in pend:
                kb.wait("tensor", "E", pend[mc % 2])
            else:
                kb.wait("tensor", "E")
            for kc in range(NC_D):
                kb.op("tensor", lambda e, psv=psv, kc=kc, mc=mc, j=j: e.matmul(
                    psv[:, 0:512], w1[:, kc * 1024 + mc * 128: kc * 1024 + (mc + 1) * 128],
                    rhs[j][:, kc * 512:(kc + 1) * 512],
                    start=(kc == 0), stop=(kc == NC_D - 1)),
                    inc=("T" if kc == NC_D - 1 else None))
            tv = kb.semc["T"]
            kb.wait("scalar", "T", tv)
            kb.op("scalar", lambda e, psv=psv, mc=mc, tt=tt: e.copy(
                xp_sb[:, mc * SP3 + 3 + tt * 512: mc * SP3 + 3 + (tt + 1) * 512],
                psv[:, 0:512]), inc="E")
            pend[mc % 2] = kb.semc["E"]
        for mc in range(NC_D):
            psv = ps[mc % 2]
            kb.wait("tensor", "E", pend[mc % 2])
            for kc in range(NC_D):
                kb.op("tensor", lambda e, psv=psv, kc=kc, mc=mc, j=j: e.matmul(
                    psv[:, 0:512], w2[:, kc * 1024 + mc * 128: kc * 1024 + (mc + 1) * 128],
                    rhs[j][:, kc * 512:(kc + 1) * 512],
                    start=(kc == 0), stop=(kc == NC_D - 1)),
                    inc=("T" if kc == NC_D - 1 else None))
            tv = kb.semc["T"]
            ep_sres(kb, psv, mc, tt, tv)
            pend[mc % 2] = kb.semc["E"]
    kb.wait("vector", "qS")
    kb.wait("scalar", "qS")
    for mc in range(NC_D):
        kb.wait("vector", "E")
        kb.seq("vector",
               lambda e, mc=mc: e.tensor_scalar(
                   acc[:, 0:S], xp_sb[:, mc * SP3: mc * SP3 + S],
                   cw[:, mc * 4: mc * 4 + 1], None, ALU.mult),
               lambda e, mc=mc: e.scalar_tensor_tensor(
                   acc[:, 0:S], xp_sb[:, mc * SP3 + 1: mc * SP3 + 1 + S],
                   cw[:, mc * 4 + 1: mc * 4 + 2], acc[:, 0:S], ALU.mult, ALU.add),
               lambda e, mc=mc: e.scalar_tensor_tensor(
                   acc[:, 0:S], xp_sb[:, mc * SP3 + 2: mc * SP3 + 2 + S],
                   cw[:, mc * 4 + 2: mc * 4 + 3], acc[:, 0:S], ALU.mult, ALU.add),
               lambda e, mc=mc: e.scalar_tensor_tensor(
                   acc[:, 0:S], xp_sb[:, mc * SP3 + 3: mc * SP3 + 3 + S],
                   cw[:, mc * 4 + 3: mc * 4 + 4], acc[:, 0:S], ALU.mult, ALU.add),
               final_inc="V")
        kb.wait("scalar", "V")
        kb.op("scalar", lambda e, mc=mc: e.activation(
            xc_sb[:, mc * S:(mc + 1) * S], acc[:, 0:S], AF.Silu,
            bias=cb[:, mc:mc + 1]), inc="S")
        kb.wait("vector", "S")
    kb.wait("sync", "S")
    kb.dma("sync", o_xc.ap().rearrange("(c p) s -> p c s", p=128),
           xc_sb[:].rearrange("p (c s) -> p c s", c=NC_D), sem="qDo")
    wx = kb.sb("sm_wx", [128, NC_D * 96], BF16)
    s3, v3 = load_weight_into(kb, wx, wxT, D, 96)
    kb.wait("tensor", s3, v3)
    kb.wait("tensor", "S")
    ep_pp = make_dram_epilogue(kb, o_pp, S, eng="vector", dtype=F32, mw=96)
    emit_matmul(kb, wx, D, 96, xc_sb, S, ep_pp, ps, mw=96)
    kb.wait("sync", "qDo")
    kb.barrier()
    if _standalone:
        return kb.build()


def build_ssm_scan(summed=False, kb=None):
    _standalone = kb is None
    if _standalone:
        kb = KB()
    xc_d = kb.inp("xc", [1024, S], BF16)
    sres_d = kb.inp("sres", [1024, S], BF16)
    pe_d = kb.inp("projp_e", [96, S])
    po_d = pe_d if summed else kb.inp("projp_o", [96, S])
    wdT = kb.inp("wdT", [64, 1024])
    bdel = kb.inp("b_delta", [1024])
    alog = kb.inp("a_log", [1024, 16])
    dpar = kb.inp("d_param", [1024])
    o_g = kb.outp("g", [1024, S], BF16)

    bmcm_d = kb.dram("ssm_bmcm_%d" % kb.uid(), [32, S], F32)
    dlt = kb.sb("sc_dlt", [64, S], BF16)
    if summed:
        kb.dma("gpsimd", dlt[:], pe_d.ap()[0:64, :])
        kb.dma("sync", bmcm_d.ap(), pe_d.ap()[64:96, :], sem="qBD")
        kb.wait("vector", "qG")
        kb.wait("tensor", "qG")
        kb.wait("gpsimd", "qBD")
    else:
        kb.push_scope()
        pa = kb.sb("sc_pa", [64, S], F32)
        pb = kb.sb("sc_pb", [64, S], F32)
        bm16 = kb.sb("sc_bm", [32, S], F32)
        kb.dma("sync", pa[:], pe_d.ap()[0:64, :])
        kb.dma("sync", pb[:], po_d.ap()[0:64, :])
        kb.wait("vector", "qS")
        kb.op("vector", lambda e: e.tensor_add(dlt[:], pa[:], pb[:]), inc="V")
        kb.wait("sync", "V")
        kb.dma("sync", pa[0:32, :], pe_d.ap()[64:96, :], sem="qP")
        kb.dma("sync", pb[0:32, :], po_d.ap()[64:96, :], sem="qP")
        kb.wait("vector", "qP")
        kb.wait("vector", "V")
        kb.op("vector", lambda e: e.tensor_add(bm16[:], pa[0:32, :], pb[0:32, :]),
              inc="V")
        kb.wait("sync", "V")
        kb.dma("sync", bmcm_d.ap(), bm16[:], sem="qBD")
        kb.wait("sync", "qBD")
        kb.barrier()
        kb.pop_scope()
    E_sb = kb.sb("sc_E", [128, NC_D * 16], F32)
    En = kb.sb("sc_En", [128, NC_D * 16], F32)
    kb.dma("sync", E_sb[:].rearrange("p (c s) -> p c s", c=NC_D),
           alog.ap().rearrange("(c p) s -> p c s", p=128))
    bd = load_bias_cols(kb, bdel, "sc_bd", 1024)
    dp = load_bias_cols(kb, dpar, "sc_dp", 1024)
    onesc = kb.sb("sc_one", [128, 1], F32)
    kb.op("vector", lambda e: e.memset(onesc[:], 1.0), inc="V")
    ones_b1 = kb.sb("sc_ob", [1, 128], BF16)
    kb.op("vector", lambda e: e.memset(ones_b1[:], 1.0), inc="V")
    kb.wait("scalar", "qS")
    kb.op("scalar", lambda e: e.activation(En[:], E_sb[:], AF.Exp), inc="S")
    kb.wait("vector", "S")
    kb.op("vector", lambda e: e.tensor_scalar_mul(En[:], En[:], -1.0), inc="V")
    wd = kb.sb("sc_wd", [64, 1024], BF16)
    smw = "qL%d" % kb.uid()
    vw = kb.dma("gpsimd", wd[:], wdT.ap(), sem=smw)
    NCG = 2
    delta_g = kb.sb("sc_delta", [128, NCG * S], BF16)
    dbx_g = kb.sb("sc_dbx", [128, NCG * S], BF16)
    xc_g = kb.sb("sc_xcg", [128, NCG * S], BF16)
    y_acc = kb.sb("sc_yacc", [128, NCG * S], F32)
    abars = [kb.sb("sc_abar", [128, S], F32) for _ in range(2)]
    abar = abars[0]
    bbar = kb.sb("sc_bbar", [128, S], BF16)
    h_sb = kb.sb("sc_h", [128, S], BF16)
    bmbs = [kb.sb("sc_bmb", [128, S], BF16) for _ in range(2)]
    cmbs = [kb.sb("sc_cmb", [128, S], BF16) for _ in range(2)]
    brow = kb.sb("sc_brow", [1, 2 * S], BF16)
    sres_t = kb.sb("sc_sres", [128, S], BF16)
    gout = kb.sb("sc_gout", [128, S], BF16)
    ps = [kb.psum("sc_ps", [128, 512]) for _ in range(2)]
    kb.wait("tensor", smw, vw)
    kb.wait("tensor", "V")
    kb.wait("tensor", "S")
    kb.sem("Sc")
    kb.sem("V2")
    _v2f = {}
    for g in range(NC_D // NCG):
        kb.wait("scalar", "V2")
        kb.wait("tensor", "Sc")
        for ci in range(NCG):
            ct = g * NCG + ci
            xv = kb.dma("sync", xc_g[:, ci * S:(ci + 1) * S],
                        xc_d.ap()[ct * 128:(ct + 1) * 128, :], sem="qXC%d" % ci)
            for tt in range(S // 512):
                psv = ps[tt % 2]
                kb.wait("tensor", "E")
                kb.op("tensor", lambda e, psv=psv, ct=ct, tt=tt: e.matmul(
                    psv[:, 0:512], wd[:, ct * 128:(ct + 1) * 128],
                    dlt[:, tt * 512:(tt + 1) * 512],
                    start=True, stop=True), inc="T")
                tv = kb.semc["T"]
                kb.wait("scalar", "T", tv)
                kb.seq("scalar",
                       lambda e, psv=psv, ct=ct: e.activation(
                           abar[:, 0:512], psv[:, 0:512], AF.Abs,
                           bias=bd[:, ct:ct + 1]),
                       lambda e: e.activation(
                           abar[:, 512:1024], abar[:, 0:512], AF.Exp, scale=-1.0),
                       lambda e: e.activation(
                           abar[:, 0:512], abar[:, 512:1024], AF.Ln,
                           bias=onesc[:, 0:1]),
                       lambda e, psv=psv, ct=ct: e.activation(
                           abar[:, 512:1024], psv[:, 0:512], AF.Relu,
                           bias=bd[:, ct:ct + 1]),
                       final_inc="S")
                kb.wait("vector", "S")
                kb.op("vector", lambda e, ci=ci, tt=tt: e.tensor_add(
                    delta_g[:, ci * S + tt * 512: ci * S + (tt + 1) * 512],
                    abar[:, 0:512], abar[:, 512:1024]), inc="E")
                kb.wait("scalar", "E")
            kb.wait("vector", "qXC%d" % ci, xv)
            kb.wait("vector", "E")
            kb.seq("vector",
                   lambda e, ci=ci: e.tensor_mul(
                       dbx_g[:, ci * S:(ci + 1) * S],
                       delta_g[:, ci * S:(ci + 1) * S],
                       xc_g[:, ci * S:(ci + 1) * S]),
                   lambda e, ci=ci, ct=ct: e.tensor_scalar(
                       y_acc[:, ci * S:(ci + 1) * S],
                       xc_g[:, ci * S:(ci + 1) * S],
                       dp[:, ct:ct + 1], None, ALU.mult),
                   final_inc="V")
        kb.wait("tensor", "V")
        kb.wait("tensor", "qBD")
        kb.wait("tensor", "E")
        for s in range(16):
            gs = g * 16 + s
            sp = gs % 2
            bmb, cmb = bmbs[sp], cmbs[sp]
            kb.wait("gpsimd", "T")
            rv = kb.dma("gpsimd", brow[0:1, 0:S], bmcm_d.ap()[s:s + 1, :],
                        sem="qBR")
            rv2 = kb.dma("gpsimd", brow[0:1, S:2 * S],
                         bmcm_d.ap()[16 + s:17 + s, :], sem="qBR")
            kb.wait("tensor", "qBR", rv2)
            kb.wait("scalar", "V2", _v2f.get((gs - 2) * 2 + 1, 0))
            for tt in range(S // 512):
                kb.wait("tensor", "Sc")
                kb.op("tensor", lambda e, tt=tt: e.matmul(
                    ps[0][:, 0:512], ones_b1[0:1, :],
                    brow[0:1, tt * 512:(tt + 1) * 512], start=True, stop=True))
                kb.op("tensor", lambda e, tt=tt: e.matmul(
                    ps[1][:, 0:512], ones_b1[0:1, :],
                    brow[0:1, S + tt * 512: S + (tt + 1) * 512],
                    start=True, stop=True), inc="T")
                tv = kb.semc["T"]
                kb.wait("scalar", "T", tv)
                kb.op("scalar", lambda e, tt=tt, bmb=bmb: e.activation(
                    bmb[:, tt * 512:(tt + 1) * 512], ps[0][:, 0:512],
                    AF.Identity))
                kb.op("scalar", lambda e, tt=tt, cmb=cmb: e.activation(
                    cmb[:, tt * 512:(tt + 1) * 512], ps[1][:, 0:512],
                    AF.Identity), inc="Sc")
            _scv = kb.semc["Sc"]
            for ci in range(NCG):
                ct = g * NCG + ci
                k2 = gs * 2 + ci
                abk = abars[k2 % 2]
                kb.wait("scalar", "V2", _v2f.get(k2 - 2, 0))
                kb.op("scalar", lambda e, ci=ci, ct=ct, s=s, abk=abk:
                      e.activation(
                    abk[:, 0:S], delta_g[:, ci * S:(ci + 1) * S], AF.Exp,
                    scale=En[:, ct * 16 + s: ct * 16 + s + 1]), inc="S")
                _sv = kb.semc["S"]
                kb.wait("vector", "S", _sv)
                kb.wait("vector", "Sc", _scv)
                kb.seq("vector",
                       lambda e, ci=ci, bmb=bmb: e.tensor_mul(
                           bbar[:, 0:S], dbx_g[:, ci * S:(ci + 1) * S],
                           bmb[:, 0:S]),
                       lambda e: e.memset(bbar[:, 0:1], 0.0),
                       lambda e, abk=abk: e.tensor_tensor_scan(
                           h_sb[:, 0:S], abk[:, 0:S], bbar[:, 0:S], 0.0,
                           ALU.mult, ALU.add),
                       lambda e, cmb=cmb: e.tensor_mul(h_sb[:, 0:S],
                                                       h_sb[:, 0:S],
                                                       cmb[:, 0:S]),
                       lambda e, ci=ci: e.tensor_add(
                           y_acc[:, ci * S:(ci + 1) * S],
                           y_acc[:, ci * S:(ci + 1) * S], h_sb[:, 0:S]),
                       final_inc="V2")
                _v2f[k2] = kb.semc["V2"]
        for ci in range(NCG):
            ct = g * NCG + ci
            sv = kb.dma("sync", sres_t[:, 0:S],
                        sres_d.ap()[ct * 128:(ct + 1) * 128, :], sem="qSR%d" % ci)
            kb.wait("vector", "qSR%d" % ci, sv)
            kb.wait("vector", "V2")
            kb.wait("vector", "qGO")
            kb.seq("vector",
                   lambda e, ci=ci: e.tensor_mul(
                       gout[:, 0:S], y_acc[:, ci * S:(ci + 1) * S],
                       sres_t[:, 0:S]), final_inc="V")
            kb.wait("sync", "V")
            kb.dma("sync", o_g.ap()[ct * 128:(ct + 1) * 128, :], gout[:, 0:S],
                   sem="qGO")
    kb.wait("sync", "qGO")
    kb.barrier()
    if _standalone:
        return kb.build()


def build_ssm_out(blend=False, kb=None):
    _standalone = kb is None
    if _standalone:
        kb = KB()
    x = kb.inp("x", [D, T])
    g_d = kb.inp("g_mytok", [2 * D, S] if blend else [2 * D, T], BF16)
    woT = kb.inp("woT", [2 * D, D])
    xo = kb.outp("x_out", [D, T])
    x_sb = kb.sb("so_x", [128, NC_D * T], F32)
    g_sb = kb.sb("so_g", [128, 16 * T], BF16)
    load_x(kb, x, x_sb)
    if blend:
        hf_col = kb.inp("hf_col", [128, 1])
        hfi_col = kb.inp("hfi_col", [128, 1])
        hfc = kb.sb("so_hf", [128, 2], F32)
        kb.dma("sync", hfc[:, 0:1], hf_col.ap())
        kb.dma("sync", hfc[:, 1:2], hfi_col.ap())
        bstg = [kb.sb("so_bs", [128, T], BF16) for _ in range(2)]
        for c16 in range(16):
            bv = kb.dma("sync", bstg[0][:, 0:T],
                        g_d.ap()[c16 * 128:(c16 + 1) * 128, 0:T], sem="qB0")
            bv1 = kb.dma("sync", bstg[1][:, 0:T],
                         g_d.ap()[c16 * 128:(c16 + 1) * 128, T:2 * T], sem="qB1")
            kb.wait("vector", "qB0", bv)
            kb.wait("vector", "qB1", bv1)
            kb.wait("vector", "qS")
            kb.seq("vector",
                   lambda e, c16=c16: e.tensor_scalar(
                       g_sb[:, c16 * T:(c16 + 1) * T], bstg[0][:, 0:T],
                       hfc[:, 1:2], None, ALU.mult),
                   lambda e, c16=c16: e.scalar_tensor_tensor(
                       g_sb[:, c16 * T:(c16 + 1) * T], bstg[1][:, 0:T],
                       hfc[:, 0:1], g_sb[:, c16 * T:(c16 + 1) * T],
                       ALU.mult, ALU.add),
                   final_inc="Vbl")
            kb.wait("sync", "Vbl")
    else:
        kb.dma("sync", g_sb[:].rearrange("p (c t) -> p c t", c=16),
               g_d.ap().rearrange("(c p) t -> p c t", p=128))
    w_sb = kb.sb("so_w", [128, 16 * D], BF16)
    sm, v = load_weight_into(kb, w_sb, woT, 2 * D, D)
    ps = [kb.psum("so_ps", [128, 512]) for _ in range(2)]
    ot = [kb.sb("so_ot", [128, 512], F32) for _ in range(2)]
    kb.wait("tensor", sm, v)
    kb.wait("tensor", "qS")
    if blend:
        kb.wait("tensor", "Vbl")
    oi = {"i": 0}

    def ep(kb2, psv, mc, tt, tv):
        i = oi["i"]
        xi = i % 2
        kb.wait("vector", "T", tv)
        if i >= 2:
            kb.wait("vector", "qOo")
        kb.op("vector", lambda e, psv=psv, mc=mc, tt=tt, xi=xi: e.tensor_add(
            ot[xi][:, 0:512], psv[:, 0:512],
            x_sb[:, mc * T + tt * 512: mc * T + (tt + 1) * 512]), inc="E")
        kb.wait("sync", "E")
        kb.dma("sync", xo.ap().rearrange("(c p) t -> c p t", p=128)
               [mc:mc + 1, :, tt * 512:(tt + 1) * 512].squeeze(0),
               ot[xi][:, 0:512], sem="qOo")
        oi["i"] += 1
    emit_matmul(kb, w_sb, 2 * D, D, g_sb, T, ep, ps)
    kb.wait("sync", "qOo")
    kb.barrier()
    if _standalone:
        return kb.build()


def np_ssm(x_full, W_in, W_out, conv_w, conv_b, A_log, D_param, W_x, W_delta,
           b_delta):
    B, L, Dm = x_full.shape
    C = 2048
    STATE = 16
    xh = np_ln(x_full)
    xpres = xh @ W_in.T
    xp, res = xpres[..., :C], xpres[..., C:]
    k = conv_w.shape[-1]
    xpad = np.concatenate([np.zeros((B, k - 1, C), np.float32), xp], axis=1)
    xc = np.zeros((B, L, C), np.float32)
    w = conv_w[:, 0, :]
    for kk in range(k):
        xc += xpad[:, kk:kk + L, :] * w[:, kk]
    xc = xc + conv_b
    xc = xc / (1 + np.exp(-xc))
    proj = xc @ W_x.T
    rank = W_delta.shape[1]
    dlt, Bm, Cm = proj[..., :rank], proj[..., rank:rank + STATE], proj[..., rank + STATE:]
    z = dlt @ W_delta.T + b_delta
    delta = np.maximum(z, 0) + np.log1p(np.exp(-np.abs(z)))
    Abar = np.exp(-delta[..., None] * np.exp(A_log))
    h = np.zeros((B, C, STATE), np.float32)
    y = np.empty((B, L, C), np.float32)
    for l in range(L):
        y[:, l] = np.einsum("bcs,bs->bc", h, Cm[:, l]) + D_param * xc[:, l]
        if l + 1 < L:
            pass
        h = Abar[:, l] * h + delta[:, l][..., None] * Bm[:, l][:, None, :] * xc[:, l][..., None]
    h = np.zeros((B, C, STATE), np.float32)
    for l in range(L):
        if l >= 1:
            h = Abar[:, l] * h + delta[:, l][..., None] * Bm[:, l][:, None, :] * xc[:, l][..., None]
        y[:, l] = np.einsum("bcs,bs->bc", h, Cm[:, l]) + D_param * xc[:, l]
    sres = res / (1 + np.exp(-res))
    return (y * sres) @ W_out.T



def build_moe_router(kb=None):
    _standalone = kb is None
    if _standalone:
        kb = KB()
    x = kb.inp("x", [D, T])
    wr = kb.inp("wr", [D, 8])
    ident = kb.inp("ident", [128, 128], BF16)
    identf = kb.inp("identf", [128, 128])
    o_tok = kb.outp("xhat_tok", [T, D], BF16)
    o_lg = kb.outp("logits", [T, 8])
    x_sb = kb.sb("mr_x", [128, NC_D * T], F32)
    xhat = kb.sb("mr_xh", [128, NC_D * T], BF16)
    id_sb = kb.sb("mr_id", [128, 128], BF16)
    idf_sb = kb.sb("mr_idf", [128, 128], F32)
    load_x(kb, x, x_sb)
    kb.dma("sync", id_sb[:], ident.ap())
    kb.dma("sync", idf_sb[:], identf.ap())
    stats = kb.sb("mr_st", [128, 16 * 2], F32)
    kb.push_scope()
    aux = alloc_ln_aux(kb, sq=xhat)
    kb.wait("vector", "qS")
    kb.wait("scalar", "qS")
    m_r, rstd_r = emit_ln(kb, x_sb, xhat, aux, want_stats=True)
    ps_c = kb.psum("mr_psc", [128, 1])
    for tt in range(16):
        for a, row in ((0, m_r), (1, rstd_r)):
            kb.wait("tensor", "Vc2")
            kb.op("tensor", lambda e, tt=tt, row=row: e.transpose(
                ps_c[:, 0:1], row[0:1, tt * 128:(tt + 1) * 128],
                idf_sb[0:1, 0:1]), inc="T")
            tv = kb.semc["T"]
            kb.wait("vector", "T", tv)
            kb.op("vector", lambda e, tt=tt, a=a: e.tensor_copy(
                stats[:, tt * 2 + a: tt * 2 + a + 1], ps_c[:, 0:1]), inc="Vc2")
    kb.barrier()
    kb.pop_scope()
    kb.push_scope()
    wr_sb = kb.sb("mr_wr", [128, NC_D * 8], F32)
    kb.dma("sync", wr_sb[:].rearrange("p (c e) -> p c e", c=NC_D),
           wr.ap().rearrange("(c p) e -> p c e", p=128), sem="qWr")
    onec = kb.sb("mr_one", [128, 1], F32)
    ones_row = kb.sb("mr_or", [1, 128], F32)
    sbc = kb.sb("mr_sbc", [128, 8], F32)
    ps_s8 = kb.psum("mr_ps8", [1, 8])
    ps_b8 = kb.psum("mr_pb8", [128, 8])
    ps_lg = [kb.psum("mr_plg", [128, 8]) for _ in range(2)]
    kb.op("vector", lambda e: e.memset(onec[:], 1.0))
    kb.op("vector", lambda e: e.memset(ones_row[:], 1.0), inc="V")
    kb.wait("tensor", "V")
    kb.wait("tensor", "qWr")
    for kc in range(NC_D):
        kb.op("tensor", lambda e, kc=kc: e.matmul(
            ps_s8[0:1, 0:8], onec[:, 0:1], wr_sb[:, kc * 8:(kc + 1) * 8],
            start=(kc == 0), stop=(kc == NC_D - 1)), inc=("T" if kc == 7 else None))
    kb.wait("vector", "T")
    kb.op("vector", lambda e: e.tensor_copy(sbc[0:1, 0:8], ps_s8[0:1, 0:8]),
          inc="V")
    kb.wait("tensor", "V")
    kb.op("tensor", lambda e: e.matmul(ps_b8[:, 0:8], ones_row[0:1, :],
                                       sbc[0:1, 0:8], start=True, stop=True),
          inc="T")
    kb.wait("vector", "T")
    kb.op("vector", lambda e: e.tensor_copy(sbc[:, 0:8], ps_b8[:, 0:8]), inc="V")
    lgt = [kb.sb("mr_lg", [128, 8], F32) for _ in range(2)]
    t2 = kb.sb("mr_t2", [128, 2], F32)
    kb.wait("vector", "Vc2")
    pend = {}
    for tt in range(16):
        psv = ps_lg[tt % 2]
        if (tt % 2) in pend:
            kb.wait("tensor", "E", pend[tt % 2])
        else:
            kb.wait("tensor", "E")
        for kc in range(NC_D):
            kb.op("tensor", lambda e, psv=psv, kc=kc, tt=tt: e.matmul(
                psv[:, 0:8],
                x_sb[:, kc * T + tt * 128: kc * T + (tt + 1) * 128],
                wr_sb[:, kc * 8:(kc + 1) * 8],
                start=(kc == 0), stop=(kc == NC_D - 1)),
                inc=("T" if kc == NC_D - 1 else None))
        tv = kb.semc["T"]
        kb.wait("vector", "T", tv)
        kb.wait("vector", "qLo")
        j = tt % 2
        kb.seq("vector",
               lambda e, tt=tt: e.tensor_scalar(
                   t2[:, 0:1], stats[:, tt * 2:tt * 2 + 1],
                   stats[:, tt * 2 + 1:tt * 2 + 2], None, ALU.mult),
               lambda e, psv=psv, tt=tt, j=j: e.tensor_scalar(
                   lgt[j][:, 0:8], psv[:, 0:8],
                   stats[:, tt * 2 + 1:tt * 2 + 2], None, ALU.mult),
               lambda e, j=j: e.tensor_scalar(
                   t2[:, 1:2], t2[:, 0:1], -1.0, None, ALU.mult),
               lambda e, j=j: e.scalar_tensor_tensor(
                   lgt[j][:, 0:8], sbc[:, 0:8], t2[:, 1:2], lgt[j][:, 0:8],
                   ALU.mult, ALU.add),
               final_inc="E")
        kb.wait("sync", "E")
        kb.dma("sync", o_lg.ap()[tt * 128:(tt + 1) * 128, :], lgt[j][:, 0:8],
               sem="qLo")
        pend[tt % 2] = kb.semc["E"]
    kb.wait("tensor", "V")
    ps_tp = kb.psum("mr_tp", [128, 128], BF16)
    emit_transpose_to_dram(kb, xhat, o_tok, T, d=D, ident=id_sb, ps_tp=ps_tp)
    kb.wait("sync", "qLo")
    kb.barrier()
    kb.pop_scope()
    if _standalone:
        return kb.build()


def build_moe_tables(local_slice=False, kb=None):
    _standalone = kb is None
    if _standalone:
        kb = KB()
    lg_d = kb.inp("logits_full", [8 * T, 8])
    ltri = kb.inp("ltri", [128, 128], BF16)
    iota8 = kb.inp("iota8", [128, 8])
    tokids = kb.inp("tokids", [128, 128])
    identf = kb.inp("identf", [128, 128])
    if local_slice:
        o_D = kb.outp("Dtab", [T, 2])
        ebase = kb.inp("ebase_col", [128, 1])
        _br = bounds_reg(T - 1)
    else:
        o_D = kb.outp("Dtab", [8 * T, 2])
        _br = bounds_reg(8 * T - 1)
    o_g = kb.outp("gidx", [8 * T, 1])
    NTT = (8 * T) // 128
    BIG = 1.0e6

    lt_sb = kb.sb("mt_lt", [128, 128], BF16)
    io8 = kb.sb("mt_io8", [128, 8], F32)
    tid = kb.sb("mt_tid", [128, 128], F32)
    idf_sb = kb.sb("mt_idf", [128, 128], F32)
    kb.dma("sync", lt_sb[:], ltri.ap())
    kb.dma("sync", io8[:], iota8.ap())
    kb.dma("sync", tid[:], tokids.ap())
    kb.dma("sync", idf_sb[:], identf.ap())
    lg = [kb.sb("mt_lg", [128, 8], F32) for _ in range(2)]
    ind_b = kb.sb("mt_ind", [128, 8], BF16)
    cols = kb.sb("mt_cols", [128, 12], F32)
    carry = kb.sb("mt_carry", [1, 8], F32)
    packs = [kb.sb("mt_pack", [128, 2], F32) for _ in range(2)]
    sidx_is = [kb.sb("mt_sidx", [128, 1], I32) for _ in range(2)]
    gcols = kb.sb("mt_gcols", [128, 128], F32)
    exps = kb.sb("mt_exps", [128, 8], F32)
    tmp8 = kb.sb("mt_tmp8", [128, 8], F32)
    ps_rs = [kb.psum("mt_psr", [128, 8]) for _ in range(2)]
    ps_c = kb.psum("mt_psc", [1, 8])
    ps_t = kb.psum("mt_pst", [128, 128])
    ones_lhs = kb.sb("mt_ol", [1, 128], F32)
    ones_col = kb.sb("mt_oc", [128, 1], BF16)
    kb.op("vector", lambda e: e.memset(ones_lhs[:], 1.0))
    kb.op("vector", lambda e: e.memset(ones_col[:], 1.0), inc="V")
    kb.wait("tensor", "V")
    zt = kb.sb("mt_z", [128, 256], F32)
    eb_sb = kb.sb("mt_eb", [128, 1], F32)
    if local_slice:
        kb.dma("sync", eb_sb[:], ebase.ap())
    kb.op("vector", lambda e: e.memset(zt[:], 0.0), inc="V")
    kb.wait("sync", "V")
    if local_slice:
        kb.dma("sync", o_D.ap().rearrange("(p a) f -> p (a f)", p=128),
               zt[:, 0:32], sem="qZ")
    else:
        kb.dma("sync", o_D.ap().rearrange("(p a) f -> p (a f)", p=128), zt[:],
               sem="qZ")
    kb.wait("gpsimd", "qZ")
    kb.wait("tensor", "qS")
    kb.wait("vector", "qS")
    _v2f = {}
    _tcf = {}
    _scf = {}
    for tt in range(NTT):
        j = tt % 2
        psr = ps_rs[tt % 2]
        pck = packs[tt % 2]
        sxi = sidx_is[tt % 2]
        if tt >= 2:
            kb.wait("sync", "V2", _v2f[tt - 2])
        lv = kb.dma("sync", lg[j][:], lg_d.ap()[tt * 128:(tt + 1) * 128, :],
                    sem="qL%d" % j)
        kb.wait("vector", "qL%d" % j, lv)
        if tt >= 1:
            kb.wait("vector", "Tc", _tcf[tt - 1])
            kb.wait("vector", "V2", _v2f[tt - 1])
        kb.seq("vector",
               lambda e, j=j: e.reduce_max(cols[:, 0:1], lg[j][:], axis=AX.X),
               lambda e, j=j: e.tensor_scalar(
                   cols[:, 1:2], cols[:, 0:1], -1.0, None, ALU.mult),
               final_inc="V")
        kb.wait("scalar", "V")
        kb.op("scalar", lambda e, j=j: e.activation(
            exps[:], lg[j][:], AF.Exp, bias=cols[:, 1:2]), inc="S")
        kb.wait("vector", "S")
        kb.seq("vector",
               lambda e: e.reduce_sum(cols[:, 2:3], exps[:], axis=AX.X),
               lambda e: e.reciprocal(cols[:, 3:4], cols[:, 2:3]),
               lambda e, j=j: e.tensor_scalar(
                   tmp8[:], lg[j][:], cols[:, 0:1], None, ALU.is_ge),
               lambda e: e.tensor_copy(ind_b[:], tmp8[:]),
               lambda e: e.tensor_mul(tmp8[:], tmp8[:], io8[:]),
               lambda e: e.reduce_max(cols[:, 4:5], tmp8[:], axis=AX.X),
               final_inc="G2")
        kb.wait("tensor", "G2")
        if tt >= 2:
            kb.wait("tensor", "V2", _v2f[tt - 2])
        kb.wait("tensor", "Vcar")
        if tt == 0:
            kb.wait("vector", "T")
            kb.op("vector", lambda e: e.memset(carry[:], 0.0), inc="Vcar")
            kb.wait("tensor", "Vcar")
        kb.op("tensor", lambda e, psr=psr: e.matmul(
            psr[:, 0:8], lt_sb[:, 0:128], ind_b[:, 0:8],
            start=True, stop=False))
        kb.op("tensor", lambda e, psr=psr: e.matmul(
            psr[:, 0:8], ones_lhs[0:1, :], carry[0:1, 0:8],
            start=False, stop=True), inc="T")
        tv = kb.semc["T"]
        kb.op("tensor", lambda e, tt=tt: e.matmul(
            ps_c[0:1, 0:8], ones_col[:, 0:1], ind_b[:, 0:8],
            start=True, stop=True), inc="Tc")
        tvc = kb.semc["Tc"]
        _tcf[tt] = tvc
        kb.wait("vector", "Tc", tvc)
        kb.seq("vector",
               lambda e: e.tensor_tensor(carry[0:1, 0:8], carry[0:1, 0:8],
                                         ps_c[0:1, 0:8], ALU.add),
               final_inc="Vcar")
        kb.wait("vector", "T", tv)
        kb.seq("vector",
               lambda e, j=j: e.tensor_scalar(
                   tmp8[:], lg[j][:], cols[:, 0:1], None, ALU.is_ge),
               lambda e, psr=psr: e.tensor_tensor(tmp8[:], tmp8[:],
                                                  psr[:, 0:8], ALU.mult),
               lambda e: e.reduce_sum(cols[:, 5:6], tmp8[:], axis=AX.X),
               lambda e: e.tensor_scalar(cols[:, 6:7], cols[:, 5:6], 2048.0,
                                         None, ALU.is_lt),
               lambda e: e.tensor_scalar(
                   cols[:, 9:10], cols[:, 4:5], 2048.0, None, ALU.mult),
               lambda e: e.tensor_add(cols[:, 7:8], cols[:, 9:10], cols[:, 5:6]),
               lambda e: e.tensor_scalar(
                   cols[:, 9:10], cols[:, 6:7], -BIG, None, ALU.mult),
               lambda e: e.scalar_tensor_tensor(
                   cols[:, 7:8], cols[:, 7:8], 1.0, cols[:, 9:10],
                   ALU.mult, ALU.add),
               lambda e: e.tensor_scalar_add(cols[:, 7:8], cols[:, 7:8], BIG),
               lambda e: e.tensor_mul(cols[:, 8:9], cols[:, 3:4], cols[:, 6:7]),
               final_inc="V2")
        _v2f[tt] = kb.semc["V2"]
        if tt >= 2:
            kb.wait("vector", "qSc", _scf[tt - 1])
        if local_slice:
            kb.wait("vector", "qS")
            kb.seq("vector",
                   lambda e, tt=tt, pck=pck: e.tensor_copy(pck[:, 0:1],
                                                           tid[:, tt:tt + 1]),
                   lambda e, pck=pck: e.tensor_copy(pck[:, 1:2], cols[:, 8:9]),
                   lambda e, tt=tt: e.tensor_copy(gcols[:, tt:tt + 1],
                                                  cols[:, 7:8]),
                   lambda e: e.tensor_sub(cols[:, 10:11], cols[:, 7:8],
                                          eb_sb[:, 0:1]),
                   lambda e, sxi=sxi: e.tensor_copy(sxi[:], cols[:, 10:11]),
                   final_inc="V3")
        else:
            kb.seq("vector",
                   lambda e, tt=tt, pck=pck: e.tensor_copy(pck[:, 0:1],
                                                           tid[:, tt:tt + 1]),
                   lambda e, pck=pck: e.tensor_copy(pck[:, 1:2], cols[:, 8:9]),
                   lambda e, sxi=sxi: e.tensor_copy(sxi[:], cols[:, 7:8]),
                   lambda e, tt=tt: e.tensor_copy(gcols[:, tt:tt + 1],
                                                  cols[:, 7:8]),
                   final_inc="V3")
        kb.wait("gpsimd", "V3")
        kb.op("gpsimd", lambda e, pck=pck, sxi=sxi: e.indirect_dma_start(
            out=o_D.ap(), out_offset=IndirectOffsetOnAxis(sxi[:], 0),
            in_=pck[:, 0:2], in_offset=None,
            bounds_check=_br(e), oob_is_err=False), inc="qSc", k=16)
        _scf[tt] = kb.semc["qSc"]
    kb.wait("tensor", "V3")
    kb.op("tensor", lambda e: e.transpose(ps_t[:, 0:128], gcols[:],
                                          idf_sb[:]), inc="T")
    kb.wait("vector", "T")
    kb.op("vector", lambda e: e.tensor_copy(gcols[:], ps_t[:, 0:128]), inc="V")
    kb.wait("sync", "V")
    kb.dma("sync", o_g.ap().rearrange("(tt p) o -> tt (p o)", p=128),
           gcols[:], sem="qGo")
    kb.wait("sync", "qGo")
    kb.wait("sync", "qSc")
    kb.barrier()
    if _standalone:
        return kb.build()





def build_moe_expert(kb=None):
    _standalone = kb is None
    if _standalone:
        kb = KB()
    dsl = kb.inp("Dslice", [T, 2])
    xtok = kb.inp("xtok_full", [8 * T, D], BF16)
    w1 = kb.inp("w1", [D, 4096])
    w2 = kb.inp("w2", [4096, D])
    ident = kb.inp("ident", [128, 128], BF16)
    o_ye = kb.outp("ye", [T, D], BF16)
    _br = bounds_reg(8 * T - 1)
    FF = 4096
    dcols = kb.sb("me_d", [128, 16 * 2], F32)
    kb.dma("sync", dcols[:].rearrange("p (tt f) -> p tt f", tt=16),
           dsl.ap().rearrange("(tt p) f -> p tt f", p=128), sem="qD",
           allow_slow_non_contiguous=True)
    idx_i = kb.sb("me_idx", [128, 1], I32)
    xstg = kb.sb("me_xs", [128, D], BF16)
    id_sb = kb.sb("me_id", [128, 128], BF16)
    kb.dma("sync", id_sb[:], ident.ap())
    xeT = kb.sb("me_xeT", [128, NC_D * T], BF16)
    ps_tp = kb.psum("me_tp", [128, 128], BF16)
    kb.wait("vector", "qD")
    kb.wait("gpsimd", "qD")
    kb.wait("tensor", "qS")
    for tt in range(16):
        kb.wait("vector", "Gx")
        kb.seq("vector",
               lambda e, tt=tt: e.tensor_copy(idx_i[:], dcols[:, 2 * tt:2 * tt + 1]),
               final_inc="V")
        kb.wait("gpsimd", "V")
        gv = kb.op("gpsimd", lambda e: e.indirect_dma_start(
            out=xstg[:], out_offset=None, in_=xtok.ap(),
            in_offset=IndirectOffsetOnAxis(idx_i[:], 0),
            bounds_check=_br(e), oob_is_err=False), inc="qGa", k=16)
        kb.wait("tensor", "qGa")
        for dc in range(NC_D):
            kb.op("tensor", lambda e, dc=dc: e.transpose(
                ps_tp[:, 0:128], xstg[:, dc * 128:(dc + 1) * 128],
                id_sb[:, 0:128]), inc="T")
            tv = kb.semc["T"]
            kb.wait("vector", "T", tv)
            kb.op("vector", lambda e, dc=dc, tt=tt: e.tensor_copy(
                xeT[:, dc * T + tt * 128: dc * T + tt * 128 + 128],
                ps_tp[:, 0:128]), inc="Gx")
            kb.wait("tensor", "Gx")
    ps = [kb.psum("me_ps", [128, 512]) for _ in range(2)]
    kb.push_scope()
    h_half = kb.sb("me_h", [128, (FF // 128) * 1024], BF16)
    w2_sb = kb.sb("me_w2", [128, (FF // 128) * D], BF16)
    s2, v2 = load_weight_into(kb, w2_sb, w2, FF, D)
    yt = [kb.sb("me_yt", [128, 512], BF16) for _ in range(2)]
    for sh in range(2):
        def ep_h(kb2, psv, mc, tt, tv, sh=sh):
            kb.wait("scalar", "T", tv)
            kb.op("scalar", lambda e, psv=psv, mc=mc, tt=tt: e.activation(
                h_half[:, mc * 1024 + tt * 512: mc * 1024 + (tt + 1) * 512],
                psv[:, 0:512], AF.Relu), inc="E")
        emit_matmul_stream(kb, w1, D, FF, xeT, T, sh * 1024, 1024, ep_h, ps,
                           wq="gpsimd")
        kb.wait("tensor", s2, v2)
        yi = {"i": 0}

        def ep_y(kb2, psv, tc, nt, tv, sh=sh):
            i = yi["i"]
            xi = i % 2
            gtc = sh * 8 + tc
            kb.wait("vector", "T", tv)
            if i >= 2:
                kb.wait("vector", "qYo")
            kb.op("vector", lambda e, psv=psv, gtc=gtc, xi=xi: e.tensor_scalar(
                yt[xi][:, 0:512], psv[:, 0:512],
                dcols[:, 2 * gtc + 1: 2 * gtc + 2], None, ALU.mult), inc="E")
            kb.wait("sync", "E")
            kb.dma("sync", o_ye.ap()[gtc * 128:(gtc + 1) * 128,
                                     nt * 512:(nt + 1) * 512],
                   yt[xi][:, 0:512], sem="qYo")
            yi["i"] += 1
        emit_matmul_tok(kb, h_half, FF, w2_sb, D, 1024, ep_y, ps)
        kb.barrier()
    kb.wait("sync", "qYo")
    kb.barrier()
    kb.pop_scope()
    if _standalone:
        return kb.build()


def build_moe_combine(gidx_indirect=False, kb=None):
    _standalone = kb is None
    if _standalone:
        kb = KB()
    x = kb.inp("x", [D, T])
    ye = kb.inp("ye_full", [8 * T, D], BF16)
    ident = kb.inp("ident", [128, 128], BF16)
    xo = kb.outp("x_out", [D, T])
    _br = bounds_reg(8 * T - 1)
    x_sb = kb.sb("mc_x", [128, NC_D * T], F32)
    id_sb = kb.sb("mc_id", [128, 128], BF16)
    gc = kb.sb("mc_g", [128, 16], F32)
    gi = kb.sb("mc_gi", [128, 1], I32)
    ystg = kb.sb("mc_ys", [128, D], BF16)
    ps_tp = kb.psum("mc_tp", [128, 128], BF16)
    load_x(kb, x, x_sb)
    kb.dma("sync", id_sb[:], ident.ap())
    if gidx_indirect:
        g_all = kb.inp("g_all", [8 * T, 1])
        mytok = kb.inp("mytok_cols", [128, 16])
        mtk = kb.sb("mc_mt", [128, 16], F32)
        mtk_i = kb.sb("mc_mti", [128, 1], I32)
        kb.dma("sync", mtk[:], mytok.ap(), sem="qGi")
        kb.wait("vector", "qGi")
        for tt in range(16):
            kb.wait("vector", "qGc")
            kb.seq("vector", lambda e, tt=tt: e.tensor_copy(
                mtk_i[:], mtk[:, tt:tt + 1]), final_inc="Vg")
            kb.wait("gpsimd", "Vg")
            kb.op("gpsimd", lambda e, tt=tt: e.indirect_dma_start(
                out=gc[:, tt:tt + 1], out_offset=None, in_=g_all.ap(),
                in_offset=IndirectOffsetOnAxis(mtk_i[:], 0),
                bounds_check=_br(e), oob_is_err=False), inc="qGc", k=16)
        kb.wait("vector", "qGc")
    else:
        gidx = kb.inp("gidx_my", [T, 1])
        kb.dma("sync", gc[:].rearrange("p (tt o) -> p tt o", tt=16),
               gidx.ap().rearrange("(tt p) o -> p tt o", p=128), sem="qGi",
               allow_slow_non_contiguous=True)
        kb.wait("vector", "qGi")
    kb.wait("vector", "qS")
    for tt in range(16):
        kb.wait("vector", "Gy")
        kb.seq("vector",
               lambda e, tt=tt: e.tensor_copy(gi[:], gc[:, tt:tt + 1]),
               lambda e: e.memset(ystg[:], 0.0),
               final_inc="V")
        kb.wait("gpsimd", "V")
        kb.op("gpsimd", lambda e: e.indirect_dma_start(
            out=ystg[:], out_offset=None, in_=ye.ap(),
            in_offset=IndirectOffsetOnAxis(gi[:], 0),
            bounds_check=_br(e), oob_is_err=False), inc="qGa", k=16)
        kb.wait("tensor", "qGa")
        for dc in range(NC_D):
            kb.op("tensor", lambda e, dc=dc: e.transpose(
                ps_tp[:, 0:128], ystg[:, dc * 128:(dc + 1) * 128],
                id_sb[:, 0:128]), inc="T")
            tv = kb.semc["T"]
            kb.wait("vector", "T", tv)
            kb.op("vector", lambda e, dc=dc, tt=tt: e.tensor_tensor(
                x_sb[:, dc * T + tt * 128: dc * T + tt * 128 + 128],
                x_sb[:, dc * T + tt * 128: dc * T + tt * 128 + 128],
                ps_tp[:, 0:128], ALU.add), inc="Gy")
            kb.wait("tensor", "Gy")
    kb.wait("sync", "Gy")
    store_x(kb, xo, x_sb)
    kb.wait("sync", "qS")
    kb.barrier()
    if _standalone:
        return kb.build()


def np_moe(x_full, Wr, W1, W2):
    b, s_, d = x_full.shape
    E, F_ = 8, 4096
    t = b * s_
    cap = t // E
    xf = np_ln(x_full).reshape(t, d)
    logits = xf @ Wr
    lm = logits.max(-1, keepdims=True)
    probs = np.exp(logits - lm)
    probs = probs / probs.sum(-1, keepdims=True)
    eidx = probs.argmax(-1)
    p_tok = probs[np.arange(t), eidx]
    onehot = np.eye(E, dtype=np.int64)[eidx]
    rank_all = np.cumsum(onehot, axis=0) - onehot
    rank = rank_all[np.arange(t), eidx]
    keep = rank < cap
    y = np.zeros((t, d), np.float32)
    for e in range(E):
        sl = np.where((eidx == e) & keep)[0]
        xe = xf[sl]
        h = np.maximum(xe @ W1[e], 0.0)
        yee = h @ W2[e]
        y[sl] = yee * p_tok[sl, None]
    return y.reshape(b, s_, d)


def ltri_matrix():
    m = np.triu(np.ones((128, 128), np.float32), 1)
    return m.astype(ml_dtypes.bfloat16)


"""Single-launch fused megakernel: all 8 sublayers + collectives in one program."""

BF = ml_dtypes.bfloat16


def build_fused(upto=8, taps=False):
    kb = KB()
    x_in = kb.inp("x", [D, T])
    ident = kb.inp("ident", [128, 128], BF16)
    identf = kb.inp("identf", [128, 128])
    sel = kb.inp("sel", [16, D])
    pos = kb.inp("pos", [1, S])
    qpos = kb.inp("qpos", [1, T])
    ltri = kb.inp("ltri", [128, 128], BF16)
    iota8 = kb.inp("iota8", [128, 8])
    tokids = kb.inp("tokids", [128, 128])
    ebase = kb.inp("ebase_col", [128, 1])
    hf_col = kb.inp("hf_col", [128, 1])
    hfi_col = kb.inp("hfi_col", [128, 1])
    mytok = kb.inp("mytok_cols", [128, 16])
    f_b1 = kb.inp("f_b1", [4096])
    f_b2 = kb.inp("f_b2", [D])
    wr = kb.inp("wr", [D, 8])
    w1e = kb.inp("w1_e", [D, 4096])
    w2e = kb.inp("w2_e", [4096, D])
    y_out = kb.outp("y", [D, T])

    dr = lambda n, s, dt=F32, sp="Local": kb.dram(n, s, dt, addr_space=sp)
    xres = dr("xres", [D, T])
    qa_d = dr("qa", [D, T], BF16)
    ka_d = dr("ka", [D, T], BF16)
    kta_d = dr("kta", [T, D], BF16)
    agKa = dr("agKa", [2 * D, T], BF16)
    agKta = dr("agKta", [2 * T, D], BF16)
    w1q_d = dr("w1q", [D, 4096], BF16)
    w2q_d = dr("w2q", [4096, D], BF16)
    qb_d = dr("qb", [D, T], BF16)
    kb_d = dr("kbq", [D, T], BF16)
    vb_d = dr("vbq", [T, D], BF16)
    agKb = dr("agKb", [2 * D, T], BF16)
    agVb = dr("agVb", [2 * T, D], BF16)
    xh_d = dr("xh", [D, T], BF16)
    agXh = dr("agXh", [2 * D, T], BF16)
    xc_d = dr("xcs", [1024, S], BF16)
    sres_d = dr("sress", [1024, S], BF16)
    pp_d = dr("pps", [96, S])
    pps_d = dr("ppsum", [96, S])
    g_d = dr("gss", [1024, S], BF16)
    agG = dr("agG", [2 * D, S], BF16)
    qc_d = dr("qc", [D, T], BF16)
    kct_d = dr("qct", [T, D], BF16)
    agKc = dr("agKc", [2 * D, T], BF16)
    agKct = dr("agKct", [2 * T, D], BF16)
    moe = []
    for u in range(2):
        moe.append(dict(
            lg=dr(f"lg{u}", [T, 8]),
            xtok=dr(f"xtok{u}", [T, D], BF16),
            agLG=dr(f"agLG{u}", [8 * T, 8], F32, "Shared"),
            agXtok=dr(f"agXtok{u}", [8 * T, D], BF16, "Shared"),
            dmy=dr(f"dmy{u}", [T, 2]),
            gall=dr(f"gall{u}", [8 * T, 1]),
            ye=dr(f"ye{u}", [T, D], BF16),
            agYe=dr(f"agYe{u}", [8 * T, D], BF16, "Shared"),
        ))

    PAIRS = [[0, 1], [2, 3], [4, 5], [6, 7]]
    ALL8 = [list(range(8))]

    def phase(pfx, fn, io, **kw):
        kb.pfx = pfx
        kb.io = dict(io)
        kb.push_scope()
        fn(kb=kb, **kw)
        kb.barrier()
        kb.pop_scope()
        kb.pfx = ""
        kb.io = {}

    def coll(kind, groups, ins, outs):
        for a, b_ in zip(ins, outs):
            kb.cc(kind, ALU.add if kind == "AllReduce" else ALU.bypass,
                  groups, [a.ap()], [b_.ap()])
        kb.barrier()

    kb.dma("sync", xres.ap(), x_in.ap())
    kb.barrier()

    def finish():
        kb.dma("sync", y_out.ap(), xres.ap(), sem="OD")
        kb.wait("sync", "OD")
        return kb.build()

    phase("a1_", build_prep_a,
          dict(x=xres, ident=ident, q_T=qa_d, k_T=ka_d, k_tok=kta_d))
    coll("AllGather", PAIRS, [ka_d, kta_d], [agKa, agKta])
    phase("a2_", build_attn_a,
          dict(x=xres, q_T=qa_d, kT_full=agKa, ktok_full=agKta, sel=sel,
               x_out=xres), ranked=True)
    if upto == 1:
        return finish()
    phase("f1_", build_ffn0,
          dict(x=xres, x_out=xres, w1q=w1q_d, w2q=w2q_d, b1=f_b1, b2=f_b2),
          ternarize=True)
    if upto == 2:
        return finish()
    phase("b1_", build_prep_b,
          dict(x=xres, q_T=qb_d, krec_T=kb_d, vrec_tok=vb_d))
    coll("AllGather", PAIRS, [kb_d, vb_d], [agKb, agVb])
    phase("b2_", build_attn_b,
          dict(x=xres, q_T=qb_d, kT_full=agKb, ktok_full=agVb, sel=sel,
               pos=pos, qpos=qpos, x_out=xres), ranked=True)
    if upto == 3:
        return finish()
    phase("f2_", build_ffn0,
          dict(x=xres, x_out=xres, w1q=w1q_d, w2q=w2q_d, b1=f_b1, b2=f_b2),
          ternarize=False)
    if upto == 4:
        return finish()
    phase("s1_", build_prep_ssm, dict(x=xres, xhat=xh_d))
    coll("AllGather", PAIRS, [xh_d], [agXh])
    phase("s2_", build_ssm_main,
          dict(xhat_full=agXh, xc=xc_d, sres=sres_d, projp=pp_d), ranked=True)
    coll("AllReduce", PAIRS, [pp_d], [pps_d])
    phase("s3_", build_ssm_scan,
          dict(xc=xc_d, sres=sres_d, projp_e=pps_d, g=g_d), summed=True)
    coll("AllGather", PAIRS, [g_d], [agG])
    if taps:
        xc_o = kb.outp("xc_o", [1024, S], BF16)
        sres_o = kb.outp("sres_o", [1024, S], BF16)
        pps_o = kb.outp("pps_o", [96, S])
        g_o = kb.outp("g_o", [1024, S], BF16)
        xh_o = kb.outp("xh_o", [2 * D, T], BF16)
        kb.dma("sync", xc_o.ap(), xc_d.ap())
        kb.dma("sync", sres_o.ap(), sres_d.ap())
        kb.dma("sync", pps_o.ap(), pps_d.ap())
        kb.dma("sync", g_o.ap(), g_d.ap())
        kb.dma("sync", xh_o.ap(), agXh.ap())
        kb.barrier()
    phase("s4_", build_ssm_out,
          dict(x=xres, g_mytok=agG, x_out=xres, hf_col=hf_col,
               hfi_col=hfi_col), blend=True)
    if upto == 5:
        return finish()

    def moe_pass(u):
        m = moe[u]
        phase(f"m{u}r_", build_moe_router,
              dict(x=xres, wr=wr, ident=ident, identf=identf,
                   xhat_tok=m["xtok"], logits=m["lg"]))
        coll("AllGather", ALL8, [m["lg"]], [m["agLG"]])
        coll("AllGather", ALL8, [m["xtok"]], [m["agXtok"]])
        phase(f"m{u}t_", build_moe_tables,
              dict(logits_full=m["agLG"], ltri=ltri, iota8=iota8,
                   tokids=tokids, identf=identf, ebase_col=ebase,
                   Dtab=m["dmy"], gidx=m["gall"]), local_slice=True)
        phase(f"m{u}e_", build_moe_expert,
              dict(Dslice=m["dmy"], xtok_full=m["agXtok"], w1=w1e, w2=w2e,
                   ident=ident, ye=m["ye"]))
        coll("AllGather", ALL8, [m["ye"]], [m["agYe"]])
        phase(f"m{u}c_", build_moe_combine,
              dict(x=xres, ye_full=m["agYe"], g_all=m["gall"],
                   mytok_cols=mytok, ident=ident, x_out=xres),
              gidx_indirect=True)

    moe_pass(0)
    if upto == 6:
        return finish()
    phase("c1_", build_prep_qkv,
          dict(x=xres, ident=ident, qkv_T=qc_d, qkv_tok=kct_d))
    coll("AllGather", PAIRS, [qc_d, kct_d], [agKc, agKct])
    phase("c2_", build_attn_c,
          dict(x=xres, q_T=qc_d, kT_full=agKc, ktok_full=agKct, sel=sel,
               x_out=xres), ranked=True)
    if upto == 7:
        return finish()
    moe_pass(1)

    return finish()


def make_in_maps(inputs):
    g = lambda n: np.asarray(inputs[n])
    x0 = g("L0_attn_a_x")
    ident = np.eye(128).astype(BF)
    identf = np.eye(128, dtype=np.float32)
    sel = sel_matrix()
    ltri = ltri_matrix()
    iota8 = np.broadcast_to(np.arange(8, dtype=np.float32), (128, 8)).copy()
    tokids = (np.arange(128, dtype=np.float32)[:, None]
              + 128.0 * np.arange(128, dtype=np.float32)[None, :]).copy()
    pos = np.arange(S, dtype=np.float32).reshape(1, S)
    W_inT = np.ascontiguousarray(g("L1_attn_a_W_in").T)
    conv_w = g("L1_attn_a_conv_w")[:, 0, :]
    WxT = np.ascontiguousarray(g("L1_attn_a_W_x").T)
    WdT = np.ascontiguousarray(g("L1_attn_a_W_delta").T)
    maps = []
    for c in range(8):
        b, hf = c // 2, c % 2
        cb = hf * 1024
        m = {
            "x": np.ascontiguousarray(x0[b, hf * T:(hf + 1) * T, :].T),
            "ident": ident, "identf": identf, "sel": sel, "pos": pos,
            "qpos": np.ascontiguousarray(pos[:, hf * T:(hf + 1) * T]),
            "ltri": ltri, "iota8": iota8, "tokids": tokids,
            "ebase_col": np.full((128, 1), c * T, np.float32),
            "hf_col": np.full((128, 1), float(hf), np.float32),
            "hfi_col": np.full((128, 1), 1.0 - hf, np.float32),
            "mytok_cols": (c * T + np.arange(128, dtype=np.float32)[:, None]
                           + 128.0 * np.arange(16, dtype=np.float32)[None, :]),
            "f_b1": g("L0_ffn_b1"), "f_b2": g("L0_ffn_b2"),
            "wr": g("L1_ffn_Wr"),
            "w1_e": np.ascontiguousarray(g("L1_ffn_W1")[c]),
            "w2_e": np.ascontiguousarray(g("L1_ffn_W2")[c]),
            "a1_wq": g("L0_attn_a_Wq"), "a1_wk": g("L0_attn_a_Wk"),
            "a1_bq": g("L0_attn_a_bq"), "a1_bk": g("L0_attn_a_bk"),
            "f1_w1t": np.ascontiguousarray(g("L0_ffn_W1").T),
            "f1_w2t": np.ascontiguousarray(g("L0_ffn_W2").T),
            "b1_wq": g("L0_attn_b_Wq"), "b1_wk": g("L0_attn_b_Wk"),
            "b1_wv": g("L0_attn_b_Wv"), "b1_wd": g("L0_attn_b_W_down_kv"),
            "b1_wuk": g("L0_attn_b_W_up_k"), "b1_wuv": g("L0_attn_b_W_up_v"),
            "b2_wo": g("L0_attn_b_Wo"),
            "s2_w_xp": np.ascontiguousarray(W_inT[:, cb:cb + 1024]),
            "s2_w_res": np.ascontiguousarray(W_inT[:, 2048 + cb:2048 + cb + 1024]),
            "s2_convw": np.ascontiguousarray(conv_w[cb:cb + 1024]),
            "s2_convb": np.ascontiguousarray(g("L1_attn_a_conv_b")[cb:cb + 1024]),
            "s2_wxT": np.ascontiguousarray(WxT[cb:cb + 1024]),
            "s3_wdT": np.ascontiguousarray(WdT[:, cb:cb + 1024]),
            "s3_b_delta": np.ascontiguousarray(g("L1_attn_a_b_delta")[cb:cb + 1024]),
            "s3_a_log": np.ascontiguousarray(g("L1_attn_a_A_log")[cb:cb + 1024]),
            "s3_d_param": np.ascontiguousarray(g("L1_attn_a_D_param")[cb:cb + 1024]),
            "s4_woT": np.ascontiguousarray(g("L1_attn_a_W_out").T),
            "c1_wqkvT": np.ascontiguousarray(g("L1_attn_b_W_qkv").T),
            "c2_wo": np.ascontiguousarray(g("L1_attn_b_W_o").T),
        }
        maps.append(m)
    return maps


def fused_kernel(upto=8, taps=False, **inputs):
    from concourse.bass_utils import run_bass_kernel_spmd
    nc = build_fused(upto, taps)
    maps = make_in_maps(inputs)
    res = run_bass_kernel_spmd(nc, maps, core_ids=list(range(8))).results
    out = np.empty((4, S, 1024), dtype=np.float32)
    for c in range(8):
        b, hf = c // 2, c % 2
        out[b, hf * T:(hf + 1) * T, :] = np.asarray(res[c]["y"]).T
    if taps:
        return out, res
    return out


def kernel(**inputs):
    return fused_kernel(upto=8, **inputs)
